# version 06
# baseline (speedup 1.0000x reference)
import numpy as np
import ml_dtypes
from contextlib import ExitStack

import concourse.bass as bass
import concourse.mybir as mybir
from concourse.bass_utils import run_bass_kernel_spmd

F32 = mybir.dt.float32
BF16 = mybir.dt.bfloat16
I32 = mybir.dt.int32
U8 = mybir.dt.uint8
AF = mybir.ActivationFunctionType
ALU = mybir.AluOpType
AX = mybir.AxisListType.X

D = 1024
HID = 3072
NH = 16
NT = 32
NQ = 16
S127 = float(np.float32(1.0 / 127.0))
EPS = 1e-5
KB = 1024

DEBUG = 0

ENGINES = ("sp", "pe", "act", "dve", "po")


class Sem:
    __slots__ = ("h", "n")

    def __init__(self, h):
        self.h = h
        self.n = 0


def build_program():
    nc = bass.Bass("TRN2", target_bir_lowering=False, debug=False)

    x_h = nc.dram_tensor("x", [NT, 128, D], F32, kind="ExternalInput").ap()
    wgT_h = nc.dram_tensor("wgT", [D, HID], I32, kind="ExternalInput").ap()
    wuT_h = nc.dram_tensor("wuT", [D, HID], I32, kind="ExternalInput").ap()
    wdT_h = nc.dram_tensor("wdT", [HID, D], I32, kind="ExternalInput").ap()
    wqkvT_h = nc.dram_tensor("wqkvT", [D, HID], F32, kind="ExternalInput").ap()
    woT_h = nc.dram_tensor("woT", [D, D], F32, kind="ExternalInput").ap()
    bqkvc_h = nc.dram_tensor("bqkvc", [128, 24], F32, kind="ExternalInput").ap()
    bvr_h = nc.dram_tensor("bvr", [1, D], F32, kind="ExternalInput").ap()
    bor_h = nc.dram_tensor("bor", [1, D], F32, kind="ExternalInput").ap()
    ghost_h = nc.dram_tensor("ghost", [1, NH], F32, kind="ExternalInput").ap()
    mask_h = nc.dram_tensor("mask", [128, 256], BF16, kind="ExternalInput").ap()
    eye_h = nc.dram_tensor("eye", [128, 128], BF16, kind="ExternalInput").ap()

    out_h = nc.dram_tensor("out", [NQ, 128, D], F32, kind="ExternalOutput").ap()

    spill_kind = "ExternalOutput" if DEBUG else "Internal"
    ln2_sp = nc.dram_tensor("ln2_sp", [NT, 128, D], BF16, kind=spill_kind).ap()
    h1_sp = nc.dram_tensor("h1_sp", [NQ, 128, D], F32, kind=spill_kind).ap()
    x2_sp = nc.dram_tensor("x2_sp", [NQ, 128, D], F32, kind=spill_kind).ap()
    ln3_sp = nc.dram_tensor("ln3_sp", [NQ, 128, D], BF16, kind=spill_kind).ap()

    with ExitStack() as stack:
        En = stack.enter_context

        AR = 201 * KB
        arena = En(nc.sbuf_tensor("arena", [128, AR], U8))[:, :]

        def carve(off_kb, shape, dtype, partitions=128):
            nb = int(np.prod(shape[1:])) * mybir.dt.size(dtype)
            off = (int(off_kb * KB) + 63) // 64 * 64
            assert off + nb <= AR, (off_kb, shape, dtype)
            ap = arena[:, off:off + nb].bitcast(dtype)
            if len(shape) == 3:
                ap = ap.rearrange("p (a b) -> p a b", a=shape[1])
            elif len(shape) == 4:
                ap = ap.rearrange("p (a b c) -> p a b c", a=shape[1], b=shape[2])
            if partitions != 128:
                ap = ap[0:partitions]
            return ap

        eye = En(nc.sbuf_tensor("eye_s", [128, 128], BF16))[:, :]
        mask = En(nc.sbuf_tensor("mask_s", [128, 256], BF16))[:, :]
        bqkvc = En(nc.sbuf_tensor("bqkvc_s", [128, 24], F32))[:, :]
        bq8c = En(nc.sbuf_tensor("bq8c", [128, 8], F32))[:, :]
        bvr = En(nc.sbuf_tensor("bvr_s", [1, D], BF16))[:, :]
        bor = En(nc.sbuf_tensor("bor_s", [1, D], BF16))[:, :]
        ghost = En(nc.sbuf_tensor("ghost_s", [1, NH], F32))[:, :]
        ones_bf = En(nc.sbuf_tensor("ones_bf", [1, 128], BF16))[:, :]
        onesf = En(nc.sbuf_tensor("onesf", [128, 64], F32))[:, :]
        stats = En(nc.sbuf_tensor("stats", [128, 32], F32))[:, :]
        tmpr = carve(154.56, [128, 512], F32)

        pb = [En(nc.psum_tensor(f"pb{i}", [128, 512], F32))[:, :] for i in range(7)]
        tpb = En(nc.psum_tensor("tpb", [128, 1024], BF16))[:, :]
        tpb3 = tpb.rearrange("p (a b) -> p a b", a=8)

        Q = {e: [] for e in ENGINES}

        def mksem(name):
            return Sem(En(nc.semaphore(name)))

        def op(eng, fn, sem=None, inc=1):
            if sem is None:
                Q[eng].append(fn)
                return None
            sem.n += inc
            v = sem.n
            Q[eng].append(lambda E, fn=fn, s=sem, i=inc: fn(E).then_inc(s.h, i))
            return v

        def wait(eng, sem, v):
            if v:
                Q[eng].append(lambda E, s=sem, v=v: E.wait_ge(s.h, v))

        bsems = {e: mksem(f"bar_{e}") for e in ENGINES}
        bar_scr = En(nc.sbuf_tensor("bar_scr", [1, 16], F32))[:, :]

        def barrier():
            vals = {}
            for e in ENGINES:
                if e == "dve":
                    vals[e] = op(e, lambda E: E.memset(bar_scr[0:1, 0:4], 0.0),
                                 sem=bsems[e])
                elif e == "act":
                    vals[e] = op(e, lambda E: E.activation(
                        out=bar_scr[0:1, 8:12], in_=bar_scr[0:1, 4:8],
                        func=AF.Identity), sem=bsems[e])
                else:
                    vals[e] = op(e, lambda E: E.nop(), sem=bsems[e])
            for e in ENGINES:
                for e2 in ENGINES:
                    if e2 != e:
                        wait(e, bsems[e2], vals[e2])

        s_small = mksem("s_small")
        sg_wg = mksem("sg_wg")
        sg_wu = mksem("sg_wu")
        sg_wd = mksem("sg_wd")
        sg_kvw = mksem("sg_kvw")
        sg_qw = mksem("sg_qw")
        sg_wo = mksem("sg_wo")
        s_x = [mksem(f"s_x{i}") for i in range(4)]
        s_h1w = [mksem(f"s_h1w{i}") for i in range(2)]
        s_ln2w = [mksem(f"s_ln2w{i}") for i in range(2)]
        s_ln2r = [mksem(f"s_ln2r{i}") for i in range(2)]
        s_lnqr = mksem("s_lnqr")
        s_h1r = [mksem(f"s_h1r{i}") for i in range(2)]
        s_x2w = [mksem(f"s_x2w{i}") for i in range(2)]
        s_ln3w = [mksem(f"s_ln3w{i}") for i in range(2)]
        s_ln3r = [mksem(f"s_ln3r{i}") for i in range(2)]
        s_x2r = [mksem(f"s_x2r{i}") for i in range(4)]
        s_out = [mksem(f"s_out{i}") for i in range(2)]
        pe = mksem("pe")
        act = mksem("act")
        dve = mksem("dve")

        op("sp", lambda E: E.dma_start(out=eye, in_=eye_h), sem=s_small, inc=16)
        op("sp", lambda E: E.dma_start(out=mask, in_=mask_h), sem=s_small, inc=16)
        op("sp", lambda E: E.dma_start(out=bqkvc, in_=bqkvc_h), sem=s_small, inc=16)
        op("sp", lambda E: E.dma_start(out=ghost, in_=ghost_h), sem=s_small, inc=16)
        op("po", lambda E: E.dma_start(out=bvr, in_=bvr_h), sem=s_small, inc=16)
        op("po", lambda E: E.dma_start(out=bor, in_=bor_h), sem=s_small, inc=16)
        v_small = s_small.n

        wgt = {"wg": carve(0, [128, 8, HID], BF16),
               "wu": carve(48, [128, 8, HID], BF16),
               "wd": carve(96, [128, 24, D], BF16)}
        op("po", lambda E, o=wgt["wg"]: E.dma_start(
            out=o, in_=wgT_h.rearrange("(c p) h -> p c h", p=128)), sem=sg_wg, inc=16)
        op("po", lambda E, o=wgt["wu"]: E.dma_start(
            out=o, in_=wuT_h.rearrange("(c p) h -> p c h", p=128)), sem=sg_wu, inc=16)
        op("po", lambda E, o=wgt["wd"]: E.dma_start(
            out=o, in_=wdT_h.rearrange("(c p) h -> p c h", p=128)), sem=sg_wd, inc=16)
        wg_load = [1]

        op("dve", lambda E: E.memset(ones_bf, 1.0))
        op("dve", lambda E: E.memset(onesf, 1.0))
        wait("dve", s_small, v_small)
        v_bq8 = op("dve", lambda E: E.tensor_scalar_mul(bq8c, bqkvc[:, 0:8], 0.125), sem=dve)

        def warmup(n=24):
            for _ in range(n):
                op("pe", lambda E: E.matmul(pb[0][:, 0:128], eye, eye,
                                            start=True, stop=True))

        ln_slot_free = [0, 0, 0, 0]
        ln_count = [0]
        sq_off = [188.0]

        def emit_ln(xa, out_ap, x_waits, out_war=()):
            r = ln_count[0] % 4
            ln_count[0] += 1
            st = stats[:, 8 * r:8 * r + 8]
            sums, nm, m2, ve, lnv, rstd, tmb = (st[:, i:i + 1] for i in range(7))
            sqs = st[:, 7:8]
            sq = carve(sq_off[0], [128, D], F32)
            for (s, v) in x_waits:
                wait("dve", s, v)
            wait("dve", act, ln_slot_free[r])
            v_sums = op("dve", lambda E, xa=xa, o=sums: E.reduce_sum(o, xa, axis=AX), sem=dve)
            for (s, v) in x_waits:
                wait("act", s, v)
            wait("act", act, ln_slot_free[r])
            v_sqs = op("act", lambda E, xa=xa, sq=sq, o=sqs: E.activation(
                sq, xa, AF.Square, accum_out=o[:, 0:1]), sem=act)
            wait("dve", dve, v_sums)
            wait("dve", act, v_sqs)
            v_nm = op("dve", lambda E, i=sums, o=nm: E.tensor_scalar_mul(o, i, -1.0 / D),
                      sem=dve)
            wait("dve", dve, v_nm)
            v_m2 = op("dve", lambda E, i=nm, o=m2: E.tensor_tensor(o, i, i, ALU.mult),
                      sem=dve)
            wait("dve", dve, v_m2)
            v_ve0 = op("dve", lambda E, i=sqs, m=m2, o=ve: E.scalar_tensor_tensor(
                o, i, 1.0 / D, m, ALU.mult, ALU.subtract), sem=dve)
            wait("dve", dve, v_ve0)
            v_ve = op("dve", lambda E, o=ve: E.tensor_scalar_add(o, o, EPS), sem=dve)
            wait("act", dve, v_ve)
            v_sr = op("act", lambda E, i=ve, o=lnv: E.activation(
                o, i, AF.Sqrt), sem=act)
            wait("dve", act, v_sr)
            v_rr = op("dve", lambda E, i=lnv, o=rstd: E.reciprocal(o, i), sem=dve)
            wait("dve", dve, v_rr)
            v_tmb = op("dve", lambda E, a=nm, b=rstd, o=tmb: E.tensor_tensor(
                o, a, b, ALU.mult), sem=dve)
            wait("act", dve, v_tmb)
            for (s_, v_) in out_war:
                wait("act", s_, v_)
            v_out = op("act", lambda E, xa=xa, o=out_ap, sc=rstd, bi=tmb: E.activation(
                o, xa, AF.Identity, bias=bi[:, 0:1], scale=sc[:, 0:1]), sem=act)
            ln_slot_free[r] = v_out
            return v_out

        tp_free = [0]

        def emit_transpose(ln_ap, lnT_dst, v_ready, rsem):
            wait("pe", rsem, v_ready) if rsem is not None else None
            wait("pe", dve, tp_free[0])
            v_tp = None
            for j in range(8):
                v_tp = op("pe", lambda E, o=tpb[:, 128 * j:128 * (j + 1)],
                          i=ln_ap[:, 128 * j:128 * (j + 1)]: E.transpose(o, i, eye),
                          sem=pe if j == 7 else None)
            wait("dve", pe, v_tp)
            v_ev = op("dve", lambda E, d=lnT_dst: E.tensor_copy(d, tpb3), sem=dve)
            tp_free[0] = v_ev
            return v_ev, v_tp

        def emit_ffn(lnT, v_lnT_ready, hp_off, sg_off, sink, war_free, wg=None):
            w_ = wgt
            hp = carve(hp_off, [128, 12, 256], BF16)
            sgb = [carve(sg_off + 0.5 * i, [128, 256], BF16) for i in range(2)]
            usb = [carve(sg_off + 1.0 + 0.5 * i, [128, 256], BF16) for i in range(2)]
            wait("pe", dve, v_lnT_ready)
            wait("pe", sg_wg, 16 * wg_load[0])
            wait("pe", sg_wu, 16 * wg_load[0])
            wait("pe", sg_wd, 16 * wg_load[0])
            v_hp = {}
            v_evac = {}
            last_down_v = 0
            for half in range(2):
                for k in range(12):
                    ht = 12 * half + k
                    if ht > 0:
                        wait("pe", act, v_evac[ht - 1][0])
                    v_g = None
                    for c in range(8):
                        v_g = op("pe", lambda E, o=w_["wg"][:, c, 128 * ht:128 * (ht + 1)],
                                 r=lnT[:, c, :], st=(c == 0), sp=(c == 7):
                                 E.matmul(pb[0][:, 0:256], o, r, start=st, stop=sp),
                                 sem=pe if c == 7 else None)
                    if ht > 0:
                        wait("pe", act, v_evac[ht - 1][1])
                    v_u = None
                    for c in range(8):
                        v_u = op("pe", lambda E, o=w_["wu"][:, c, 128 * ht:128 * (ht + 1)],
                                 r=lnT[:, c, :], st=(c == 0), sp=(c == 7):
                                 E.matmul(pb[1][:, 0:256], o, r, start=st, stop=sp),
                                 sem=pe if c == 7 else None)
                    wait("act", pe, v_g)
                    if ht >= 2:
                        wait("act", dve, v_hp[ht - 2])
                    v_sg = op("act", lambda E, o=sgb[ht % 2]: E.activation(
                        o, pb[0][:, 0:256], AF.Silu, scale=S127), sem=act)
                    wait("act", pe, v_u)
                    v_us = op("act", lambda E, o=usb[ht % 2]: E.activation(
                        o, pb[1][:, 0:256], AF.Identity, scale=0.5 * S127 * S127), sem=act)
                    v_evac[ht] = (v_sg, v_us)
                    wait("dve", act, v_us)
                    if half == 1:
                        wait("dve", pe, last_down_v)
                    v_hp[ht] = op("dve", lambda E, o=hp[:, ht % 12, :], a=sgb[ht % 2],
                                  b=usb[ht % 2]: E.tensor_tensor(o, a, b, ALU.mult), sem=dve)
                for k in range(12):
                    ht = 12 * half + k
                    wait("pe", dve, v_hp[ht])
                    if ht == 0:
                        for v in war_free:
                            wait("pe", dve, v)
                    st_ = (ht == 0)
                    sp_ = (ht == 23)
                    for tt in range(2):
                        lo, hi = [(pb[4], pb[5]), (pb[2], pb[3])][tt]
                        op("pe", lambda E, l=hp[:, ht % 12, 128 * tt:128 * (tt + 1)],
                           r=w_["wd"][:, ht, 0:512], o=lo, st=st_, sp=sp_:
                           E.matmul(o[:, 0:512], l, r, start=st, stop=sp))
                        last_down_v = op("pe", lambda E, l=hp[:, ht % 12, 128 * tt:128 * (tt + 1)],
                                         r=w_["wd"][:, ht, 512:1024], o=hi, st=st_, sp=sp_:
                                         E.matmul(o[:, 0:512], l, r, start=st, stop=sp),
                                         sem=pe)
            frees = []
            for tt in range(2):
                lo, hi = [(pb[4], pb[5]), (pb[2], pb[3])][tt]
                frees.append(sink(tt, lo, hi, last_down_v))
            return frees

        x_ring = [carve(144 + 4 * i, [128, D], F32) for i in range(4)]
        ln1_ring = [carve(193 + 2 * i, [128, D], BF16) for i in range(4)]
        lnT_ring = [carve(160 + 4 * i, [128, 8, 256], BF16) for i in range(2)]
        ln2_ring = [carve(176 + 2 * i, [128, D], BF16) for i in range(2)]
        h1_ring = [carve(180 + 4 * i, [128, D], F32) for i in range(2)]

        ms_h1add = {}
        ms_ln2 = {}
        ms_h1w = {}
        ms_ln2w = {}
        xld = {}
        v_ln1s = {}
        v_tp1 = {}
        war_free = []

        def p1_tile_head(t):
            if t >= 4:
                wait("sp", dve, ms_h1add[t - 4])
            xld[t] = op("sp", lambda E, o=x_ring[t % 4], i=x_h[t]: E.dma_start(out=o, in_=i),
                        sem=s_x[t % 4], inc=16)
            v_ln1s[t] = emit_ln(x_ring[t % 4], ln1_ring[t % 4], [(s_x[t % 4], xld[t])],
                                out_war=([(pe, v_tp1[t - 4])] if t >= 4 else ()))

        wait("pe", s_small, v_small)
        for t in (0, 1):
            p1_tile_head(t)
        for w in range(NT // 2):
            for t in (2 * w + 2, 2 * w + 3):
                if t < NT:
                    p1_tile_head(t)
            v_lnT_w = None
            for t in (2 * w, 2 * w + 1):
                v_lnT_w, v_tp1[t] = emit_transpose(
                    ln1_ring[t % 4],
                    lnT_ring[w % 2][:, :, 128 * (t % 2):128 * (t % 2 + 1)], v_ln1s[t], act)
            if w == 0:
                warmup()
            t0 = 2 * w

            def sink(tt, lo, hi, v_pe, t0=t0):
                tl = t0 + tt
                wait("dve", pe, v_pe)
                wait("dve", s_x[tl % 4], xld[tl])
                if tl >= 2:
                    wait("dve", act, ms_ln2[tl - 2])
                    if (tl - 2) % 2 == 1:
                        wait("dve", s_h1w[((tl - 2) // 2) % 2], ms_h1w[tl - 2])
                op("dve", lambda E, o=h1_ring[tl % 2][:, 0:512], a=lo[:, 0:512],
                   b=x_ring[tl % 4][:, 0:512]: E.tensor_tensor(o, a, b, ALU.add))
                v_h1 = op("dve", lambda E, o=h1_ring[tl % 2][:, 512:1024], a=hi[:, 0:512],
                          b=x_ring[tl % 4][:, 512:1024]: E.tensor_tensor(o, a, b, ALU.add),
                          sem=dve)
                ms_h1add[tl] = v_h1
                if tl % 2 == 1:
                    si = tl // 2
                    wait("sp", dve, v_h1)
                    if si >= 2:
                        wait("sp", s_h1w[si % 2], 16 * (si // 2))
                    ms_h1w[tl] = op("sp", lambda E, o=h1_sp[si], i=h1_ring[tl % 2]:
                                    E.dma_start(out=o, in_=i), sem=s_h1w[si % 2], inc=16)
                v_ln2 = emit_ln(h1_ring[tl % 2], ln2_ring[tl % 2], [(dve, v_h1)],
                                out_war=([(s_ln2w[tl % 2], ms_ln2w[tl - 2])]
                                         if tl >= 2 else ()))
                ms_ln2[tl] = v_ln2
                wait("sp", act, v_ln2)
                ms_ln2w[tl] = op("sp", lambda E, o=ln2_sp[tl], i=ln2_ring[tl % 2]:
                                 E.dma_start(out=o, in_=i), sem=s_ln2w[tl % 2], inc=16)
                return v_h1

            war_free = emit_ffn(lnT_ring[w % 2], v_lnT_w, 170.0, 168.0, sink, war_free)

        barrier()
        ln2w_tot = (s_ln2w[0].n, s_ln2w[1].n)
        h1w_tot = (s_h1w[0].n, s_h1w[1].n)

        K_sb = carve(0, [128, 8, 4096], BF16)
        V_sb = carve(64, [128, NT, NH, 65], BF16)
        kvw = carve(130.56, [128, 8, 2048], BF16)
        ln2w_ring = [carve(162.56 + 8 * i, [128, 4, D], BF16) for i in range(2)]
        ln2T_ring = [carve(178.56 + 8 * i, [128, 8, 512], BF16) for i in range(2)]

        op("po", lambda E: E.dma_start(
            out=kvw, in_=wqkvT_h[:, 1024:3072].rearrange("(c p) h -> p c h", p=128)),
            sem=sg_kvw, inc=16)
        v_vones = op("dve", lambda E: E.memset(V_sb[:, :, :, 64:65], 1.0), sem=dve)

        wait("sp", s_ln2w[0], ln2w_tot[0])
        wait("sp", s_ln2w[1], ln2w_tot[1])
        warmup()
        ms_kevac = {}
        ms_vevac = {}
        v_tr_last = {}
        for w in range(8):
            if w >= 2:
                wait("sp", pe, v_tr_last[w - 2])
            ld2 = op("sp", lambda E, o=ln2w_ring[w % 2],
                     i=ln2_sp[4 * w:4 * w + 4].rearrange("t p f -> p t f"):
                     E.dma_start(out=o, in_=i), sem=s_ln2r[w % 2], inc=16)
            v_lnT_w = None
            for tl in range(4):
                _, v_tp = emit_transpose(
                    ln2w_ring[w % 2][:, tl, :],
                    ln2T_ring[w % 2][:, :, 128 * tl:128 * (tl + 1)], ld2, s_ln2r[w % 2])
                v_tr_last[w] = v_tp
                v_lnT_w = tp_free[0]
            wait("pe", dve, v_lnT_w)
            wait("pe", sg_kvw, 16)
            wait("act", s_small, v_small)
            for j in range(8):
                bk = pb[j % 3]
                if j >= 3:
                    wait("pe", act, ms_kevac[(w, j - 3)])
                elif w > 0:
                    wait("pe", act, ms_kevac[(w - 1, j + 6 if j <= 1 else 5)])
                v_k = None
                for c in range(8):
                    v_k = op("pe", lambda E, o=bk[:, 0:512], l=kvw[:, c, 128 * j:128 * (j + 1)],
                             r=ln2T_ring[w % 2][:, c, :], st=(c == 0), sp=(c == 7):
                             E.matmul(o, l, r, start=st, stop=sp), sem=pe if c == 7 else None)
                wait("act", pe, v_k)
                ms_kevac[(w, j)] = op("act", lambda E, o=K_sb[:, j, 512 * w:512 * (w + 1)],
                                      i=bk[:, 0:512], b=bqkvc[:, 8 + j:9 + j]:
                                      E.activation(o, i, AF.Identity, bias=b[:, 0:1]),
                                      sem=act)
            for tl in range(4):
                t = 4 * w + tl
                if t >= 1:
                    wait("pe", act, ms_vevac[t - 1])
                wait("pe", s_small, v_small)
                v_v = None
                for c in range(8):
                    for hh in range(2):
                        op("pe", lambda E, o=pb[4 + hh][:, 0:512],
                           l=ln2T_ring[w % 2][:, c, 128 * tl:128 * (tl + 1)],
                           r=kvw[:, c, 1024 + 512 * hh:1024 + 512 * (hh + 1)], st=(c == 0):
                           E.matmul(o, l, r, start=st, stop=False))
                for hh in range(2):
                    v_v = op("pe", lambda E, o=pb[4 + hh][:, 0:512], l=ones_bf[0:1, 0:128],
                             r=bvr[0:1, 512 * hh:512 * (hh + 1)]:
                             E.matmul(o, l, r, start=False, stop=True),
                             sem=pe if hh == 1 else None)
                wait("act", pe, v_v)
                wait("act", dve, v_vones)
                op("act", lambda E, o=V_sb[:, t, 0:8, 0:64],
                   i=pb[4][:, 0:512].rearrange("p (a b) -> p a b", a=8):
                   E.activation(o, i, AF.Identity))
                ms_vevac[t] = op("act", lambda E, o=V_sb[:, t, 8:16, 0:64],
                                 i=pb[5][:, 0:512].rearrange("p (a b) -> p a b", a=8):
                                 E.activation(o, i, AF.Identity), sem=act)

        barrier()

        QA = carve(162.56, [128, NQ, 8, 128], BF16)
        qw = carve(130.56, [128, 8, 1024], BF16)
        lnq_ring = carve(146.56, [128, 4, D], BF16)
        lnqT = carve(154.56, [128, 8, 512], BF16)
        op("po", lambda E: E.dma_start(
            out=qw, in_=wqkvT_h[:, 0:1024].rearrange("(c p) h -> p c h", p=128)),
            sem=sg_qw, inc=16)
        ln2_odd = ln2_sp.rearrange("(a two) p f -> a two p f", two=2)
        warmup()
        ms_qevac = {}
        v_trq_last = 0
        for w in range(4):
            if w >= 1:
                wait("sp", pe, v_trq_last)
            ldq = op("sp", lambda E, o=lnq_ring,
                     i=ln2_odd[4 * w:4 * w + 4, 1:2].rearrange("t one p f -> p (t one) f"):
                     E.dma_start(out=o, in_=i), sem=s_lnqr, inc=16)
            v_lnT_w = None
            for tl in range(4):
                _, v_tp = emit_transpose(lnq_ring[:, tl, :],
                                         lnqT[:, :, 128 * tl:128 * (tl + 1)], ldq, s_lnqr)
                v_trq_last = v_tp
                v_lnT_w = tp_free[0]
            wait("pe", dve, v_lnT_w)
            wait("pe", sg_qw, 16)
            wait("act", dve, v_bq8)
            for j in range(8):
                bk = pb[j % 3]
                if j >= 3:
                    wait("pe", act, ms_qevac[(w, j - 3)])
                elif w > 0:
                    wait("pe", act, ms_qevac[(w - 1, j + 6 if j <= 1 else 5)])
                v_q = None
                for c in range(8):
                    v_q = op("pe", lambda E, o=bk[:, 0:512], l=qw[:, c, 128 * j:128 * (j + 1)],
                             r=lnqT[:, c, :], st=(c == 0), sp=(c == 7):
                             E.matmul(o, l, r, start=st, stop=sp), sem=pe if c == 7 else None)
                wait("act", pe, v_q)
                ms_qevac[(w, j)] = op("act", lambda E, o=QA[:, 4 * w:4 * w + 4, j, :],
                                      i=bk[:, 0:512].rearrange("p (a b) -> p a b", a=4),
                                      b=bq8c[:, j:j + 1]:
                                      E.activation(o, i, AF.Identity, bias=b[:, 0:1],
                                                   scale=0.125), sem=act)

        barrier()

        wo = carve(130.56, [128, 8, D], BF16)
        zbuf = [carve(146.56 + 1.0 * i, [128, 512], BF16) for i in range(4)]
        rbs_ring = [carve(150.56 + 2.0 * i, [128, 512], F32, partitions=64) for i in range(2)]
        op("po", lambda E: E.dma_start(
            out=wo, in_=woT_h.rearrange("(c p) d -> p c d", p=128)), sem=sg_wo, inc=16)

        st_lo = [pb[0], pb[1]]
        st_hi = [pb[6], pb[2]]
        a_banks = [pb[3], pb[4]]
        rbb = pb[5]
        st_rot = [0]
        z_rot = [0]
        ms_exp = {}
        ms_zcons = {}
        ms_afree = [0, 0]
        ms_rbfree = [0]

        wait("pe", s_small, v_small)
        wait("dve", s_small, v_small)
        warmup()

        def emit_qk(p, g, t, n256):
            j = g
            s0 = 2 * p
            si = st_rot[0]
            st_rot[0] += 1
            stl = st_lo[si % 2]
            sth = st_hi[si % 2]
            zi = z_rot[0]
            z_rot[0] += 1
            zb = zbuf[zi % 4]
            if si >= 2:
                wait("pe", act, ms_exp[si - 2])
            v_st = None
            for hh in range(2):
                base = 64 * hh
                if n256:
                    qs0, qs1 = s0, s0 + 2
                    ncol = 256
                else:
                    qs0, qs1 = s0 + 1, s0 + 2
                    ncol = 128
                v_st = op("pe", lambda E, o=(stl if hh == 0 else sth)[:, 0:ncol],
                          l=K_sb[base:base + 64, j, 128 * t:128 * (t + 1)],
                          r=QA[base:base + 64, qs0:qs1, j, :]:
                          E.matmul(o, l, r, start=True, stop=True),
                          sem=pe if hh == 1 else None)
            wait("act", pe, v_st)
            if zi >= 4:
                wait("act", pe, ms_zcons[zi - 4])
            half = 256 if n256 else 128
            op("act", lambda E, o=zb[:, 0:half], i=stl[:, 0:half]:
               E.activation(o, i, AF.Exp))
            v_z = op("act", lambda E, o=zb[:, half:2 * half], i=sth[:, 0:half]:
                     E.activation(o, i, AF.Exp), sem=act)
            ms_exp[si] = v_z
            gate_sem, gate_v = act, v_z
            masked = (t >= 4 * p)
            if masked:
                if n256:
                    mc = mask[:, 0:128] if t == 4 * p else mask[:, 128:256]
                    cols = [(0, 128), (256, 384)]
                else:
                    mc = mask[:, 0:128] if t == 4 * p + 2 else mask[:, 128:256]
                    cols = [(0, 128), (128, 256)]
                wait("dve", act, v_z)
                v_m = None
                for (c0, c1) in cols:
                    v_m = op("dve", lambda E, o=zb[:, c0:c1], m=mc:
                             E.tensor_tensor(o, o, m, ALU.mult),
                             sem=dve if (c0, c1) == cols[-1] else None)
                gate_sem, gate_v = dve, v_m
            return (zi, zb, gate_sem, gate_v, n256)

        def emit_sv(p, g, t, gate, first, last):
            zi, zb, gate_sem, gate_v, n256 = gate
            abank = a_banks[g % 2]
            wait("pe", gate_sem, gate_v)
            if first:
                wait("pe", dve, ms_afree[g % 2])
            v_sv = None
            for hh in range(2):
                h = 2 * g + hh
                if n256:
                    ac, zc, ncol = 256 * hh, 256 * hh, 256
                else:
                    ac, zc, ncol = 256 * hh + 128, 128 * hh, 128
                v_sv = op("pe", lambda E, o=abank[0:65, ac:ac + ncol],
                          l=V_sb[:, t, h, 0:65], r=zb[:, zc:zc + ncol],
                          st=(first and hh == 0), sp=last:
                          E.matmul(o, l, r, start=st, stop=sp),
                          sem=pe if hh == 1 else None)
            ms_zcons[zi] = v_sv
            return v_sv

        def emit_evac(p, g, v_sv):
            abank = a_banks[g % 2]
            ri = g % 2
            wait("dve", pe, v_sv)
            v_gh = None
            for hh in range(2):
                h = 2 * g + hh
                v_gh = op("dve", lambda E, o=tmpr[64:65, 256 * hh:256 * hh + 256],
                          i=abank[64:65, 256 * hh:256 * hh + 256], gh=ghost[0:1, h:h + 1]:
                          E.tensor_scalar(o, i, gh, None, ALU.add),
                          sem=dve if hh == 1 else None)
            wait("dve", dve, v_gh)
            v_rr = op("dve", lambda E: E.reciprocal(tmpr[64:65, 0:512],
                                                    tmpr[64:65, 0:512]), sem=dve)
            wait("pe", dve, v_rr)
            wait("pe", dve, ms_rbfree[0])
            v_rb = op("pe", lambda E, o=rbb[0:64, 0:512]: E.matmul(
                o, onesf[64:65, 0:64], tmpr[64:65, 0:512], start=True, stop=True), sem=pe)
            wait("dve", pe, v_rb)
            v_c = op("dve", lambda E, o=rbs_ring[ri], i=rbb[0:64, 0:512]:
                     E.tensor_copy(o, i), sem=dve)
            s0 = 2 * p
            v_a = None
            for hh in range(2):
                dst = QA[64 * hh:64 * hh + 64, s0:s0 + 2, g, :]
                v_a = op("dve", lambda E, o=dst,
                         a=abank[0:64, 256 * hh:256 * hh + 256].rearrange(
                             "p (a b) -> p a b", a=2),
                         b=rbs_ring[ri][0:64, 256 * hh:256 * hh + 256].rearrange(
                             "p (a b) -> p a b", a=2):
                         E.tensor_tensor(o, a, b, ALU.mult), sem=dve)
            ms_afree[g % 2] = v_a
            ms_rbfree[0] = v_c

        for p in range(8):
            T = 4 * p + 4
            visits = [(t, t < 4 * p + 2) for t in range(T)]
            for a in range(4):
                g0, g1 = 2 * a, 2 * a + 1
                gates0 = {0: emit_qk(p, g0, *visits[0])}
                gates1 = {0: emit_qk(p, g1, *visits[0])}
                v_sv0 = v_sv1 = None
                for i in range(T):
                    if i + 1 < T:
                        gates0[i + 1] = emit_qk(p, g0, *visits[i + 1])
                        gates1[i + 1] = emit_qk(p, g1, *visits[i + 1])
                    t, _ = visits[i]
                    v_sv0 = emit_sv(p, g0, t, gates0.pop(i), i == 0, i == T - 1)
                    v_sv1 = emit_sv(p, g1, t, gates1.pop(i), i == 0, i == T - 1)
                emit_evac(p, g0, v_sv0)
                emit_evac(p, g1, v_sv1)

        barrier()

        h1in_ring = [carve(0 + 4 * i, [128, D], F32) for i in range(2)]
        x2_ring = [carve(8 + 4 * i, [128, D], F32) for i in range(2)]
        ln3_ring = [carve(146.56 + 2 * i, [128, D], BF16) for i in range(2)]
        sq_off[0] = 150.56
        wgt2 = {"wg": carve(16, [128, 8, HID], BF16),
                "wu": carve(64, [128, 8, HID], BF16)}
        op("po", lambda E, o=wgt2["wg"]: E.dma_start(
            out=o, in_=wgT_h.rearrange("(c p) h -> p c h", p=128)), sem=sg_wg, inc=16)
        op("po", lambda E, o=wgt2["wu"]: E.dma_start(
            out=o, in_=wuT_h.rearrange("(c p) h -> p c h", p=128)), sem=sg_wu, inc=16)

        ms_x2 = {}
        ms_ln3 = {}
        ms_ln3w = {}
        wait("pe", sg_wo, 16)
        warmup()
        for s in range(NQ):
            if s >= 2:
                wait("sp", dve, ms_x2[s - 2])
                wait("sp", s_h1r[s % 2], 16 * (s // 2))
            if s == 0:
                wait("sp", s_h1w[0], h1w_tot[0])
                wait("sp", s_h1w[1], h1w_tot[1])
            ldh = op("sp", lambda E, o=h1in_ring[s % 2], i=h1_sp[s]: E.dma_start(out=o, in_=i),
                     sem=s_h1r[s % 2], inc=16)
            if s >= 1:
                wait("pe", dve, ms_x2[s - 1])
            v_wo = None
            for half in range(2):
                bank = pb[4 + half]
                for c in range(8):
                    op("pe", lambda E, o=bank[:, 0:512], l=QA[:, s, c, :],
                       r=wo[:, c, 512 * half:512 * (half + 1)], st=(c == 0):
                       E.matmul(o, l, r, start=st, stop=False))
                v_wo = op("pe", lambda E, o=bank[:, 0:512], l=ones_bf[0:1, 0:128],
                          r=bor[0:1, 512 * half:512 * (half + 1)]:
                          E.matmul(o, l, r, start=False, stop=True), sem=pe)
            wait("dve", pe, v_wo)
            wait("dve", s_h1r[s % 2], ldh)
            if s >= 2:
                wait("dve", act, ms_ln3[s - 2])
                wait("dve", s_x2w[s % 2], 16 * (s // 2))
            op("dve", lambda E, o=x2_ring[s % 2][:, 0:512], a=pb[4][:, 0:512],
               b=h1in_ring[s % 2][:, 0:512]: E.tensor_tensor(o, a, b, ALU.add))
            v_x2 = op("dve", lambda E, o=x2_ring[s % 2][:, 512:1024], a=pb[5][:, 0:512],
                      b=h1in_ring[s % 2][:, 512:1024]: E.tensor_tensor(o, a, b, ALU.add),
                      sem=dve)
            ms_x2[s] = v_x2
            wait("sp", dve, v_x2)
            op("sp", lambda E, o=x2_sp[s], i=x2_ring[s % 2]: E.dma_start(out=o, in_=i),
               sem=s_x2w[s % 2], inc=16)
            v_ln3 = emit_ln(x2_ring[s % 2], ln3_ring[s % 2], [(dve, v_x2)],
                            out_war=([(s_ln3w[s % 2], ms_ln3w[s - 2])] if s >= 2 else ()))
            ms_ln3[s] = v_ln3
            wait("sp", act, v_ln3)
            ms_ln3w[s] = op("sp", lambda E, o=ln3_sp[s], i=ln3_ring[s % 2]:
                            E.dma_start(out=o, in_=i), sem=s_ln3w[s % 2], inc=16)

        barrier()
        x2w_tot = (s_x2w[0].n, s_x2w[1].n)
        ln3w_tot = (s_ln3w[0].n, s_ln3w[1].n)

        wgt2["wd"] = carve(112, [128, 24, D], BF16)
        op("po", lambda E, o=wgt2["wd"]: E.dma_start(
            out=o, in_=wdT_h.rearrange("(c p) h -> p c h", p=128)), sem=sg_wd, inc=16)
        wgt = wgt2
        wg_load[0] = 2
        x2in_ring = [carve(0 + 4 * i, [128, D], F32) for i in range(4)]
        out_ring = [carve(160 + 4 * i, [128, D], F32) for i in range(2)]
        ln3in_ring = [carve(168 + 4 * i, [128, 2, D], BF16) for i in range(2)]
        lnT5_ring = [carve(176 + 4 * i, [128, 8, 256], BF16) for i in range(2)]

        war_free = []
        ms_out = {}
        v_tr5 = {}
        xld5 = {}
        warmup()
        for w in range(8):
            t0 = 2 * w
            if w == 0:
                wait("sp", s_ln3w[0], ln3w_tot[0])
                wait("sp", s_ln3w[1], ln3w_tot[1])
                wait("sp", s_x2w[0], x2w_tot[0])
                wait("sp", s_x2w[1], x2w_tot[1])
            if w >= 2:
                wait("sp", pe, v_tr5[w - 2])
                wait("sp", s_ln3r[w % 2], 16 * (w // 2))
            ldl = op("sp", lambda E, o=ln3in_ring[w % 2],
                     i=ln3_sp[t0:t0 + 2].rearrange("t p f -> p t f"):
                     E.dma_start(out=o, in_=i), sem=s_ln3r[w % 2], inc=16)
            for tt in range(2):
                s = t0 + tt
                if s >= 4:
                    wait("sp", dve, ms_out[s - 4])
                    wait("sp", s_x2r[s % 4], 16 * (s // 4))
                xld5[s] = op("sp", lambda E, o=x2in_ring[s % 4], i=x2_sp[s]:
                             E.dma_start(out=o, in_=i), sem=s_x2r[s % 4], inc=16)
            v_lnT_w = None
            for tt in range(2):
                _, v_tp = emit_transpose(ln3in_ring[w % 2][:, tt, :],
                                         lnT5_ring[w % 2][:, :, 128 * tt:128 * (tt + 1)],
                                         ldl, s_ln3r[w % 2])
                v_tr5[w] = v_tp
                v_lnT_w = tp_free[0]

            def sink5(tt, lo, hi, v_pe, t0=t0):
                s = t0 + tt
                wait("dve", pe, v_pe)
                wait("dve", s_x2r[s % 4], xld5[s])
                if s >= 2:
                    wait("dve", s_out[s % 2], 16 * (s // 2))
                op("dve", lambda E, o=out_ring[s % 2][:, 0:512], a=lo[:, 0:512],
                   b=x2in_ring[s % 4][:, 0:512]: E.tensor_tensor(o, a, b, ALU.add))
                v_o = op("dve", lambda E, o=out_ring[s % 2][:, 512:1024], a=hi[:, 0:512],
                         b=x2in_ring[s % 4][:, 512:1024]: E.tensor_tensor(o, a, b, ALU.add),
                         sem=dve)
                ms_out[s] = v_o
                wait("sp", dve, v_o)
                op("sp", lambda E, o=out_h[s], i=out_ring[s % 2]:
                   E.dma_start(out=o, in_=i), sem=s_out[s % 2], inc=16)
                return v_o

            war_free = emit_ffn(lnT5_ring[w % 2], v_lnT_w, 186.0, 184.0, sink5, war_free)

        wait("sp", s_out[0], s_out[0].n)
        wait("sp", s_out[1], s_out[1].n)

        with nc.Block() as block:
            block.sync(lambda e, ops=Q["sp"]: [f(e) for f in ops])
            block.tensor(lambda e, ops=Q["pe"]: [f(e) for f in ops])
            block.scalar(lambda e, ops=Q["act"]: [f(e) for f in ops])
            block.vector(lambda e, ops=Q["dve"]: [f(e) for f in ops])
            block.gpsimd(lambda e, ops=Q["po"]: [f(e) for f in ops])

    return nc


def _stage_inputs(inputs):
    x = np.asarray(inputs["L0_attn_inputs"], dtype=np.float32)
    wqkv = np.asarray(inputs["L0_attn_Wqkv_w"], dtype=np.float32)
    bqkv = np.asarray(inputs["L0_attn_Wqkv_b"], dtype=np.float32)
    wo_w = np.asarray(inputs["L0_attn_Wo_w"], dtype=np.float32)
    bo = np.asarray(inputs["L0_attn_Wo_b"], dtype=np.float32)
    ghost = np.asarray(inputs["L0_attn_ghost"], dtype=np.float32)
    wg = np.asarray(inputs["L0_ffn_w_gate"], dtype=np.int32)
    wu = np.asarray(inputs["L0_ffn_w_up"], dtype=np.int32)
    wdn = np.asarray(inputs["L0_ffn_w_down"], dtype=np.int32)

    shared = {
        "wgT": np.ascontiguousarray(wg.T),
        "wuT": np.ascontiguousarray(wu.T),
        "wdT": np.ascontiguousarray(wdn.T),
        "wqkvT": np.ascontiguousarray(wqkv.T),
        "woT": np.ascontiguousarray(wo_w.T),
        "bqkvc": np.ascontiguousarray(bqkv.reshape(24, 128).T),
        "bvr": np.ascontiguousarray(bqkv[2048:].reshape(1, D)),
        "bor": np.ascontiguousarray(bo.reshape(1, D)),
        "ghost": np.ascontiguousarray(ghost.reshape(1, NH)),
        "eye": np.eye(128, dtype=ml_dtypes.bfloat16),
    }
    tri = np.triu(np.ones((128, 128), dtype=np.float32)).astype(ml_dtypes.bfloat16)
    ones = np.ones((128, 128), dtype=ml_dtypes.bfloat16)
    zeros = np.zeros((128, 128), dtype=ml_dtypes.bfloat16)
    mask_u = np.ascontiguousarray(np.concatenate([ones, tri], axis=1))
    mask_v = np.ascontiguousarray(np.concatenate([zeros, tri], axis=1))

    perm_v = np.arange(NT)
    perm_v[0::2] += 1
    perm_v[1::2] -= 1

    in_maps = []
    for c in range(8):
        b = c // 2
        xt = x[b].reshape(NT, 128, D)
        if c % 2 == 1:
            xt = xt[perm_v]
        m = dict(shared)
        m["x"] = np.ascontiguousarray(xt)
        m["mask"] = mask_u if c % 2 == 0 else mask_v
        in_maps.append(m)
    return in_maps


def _assemble(results):
    full = np.empty((4, NT * 128, D), dtype=np.float32)
    for c in range(8):
        b = c // 2
        o = np.asarray(results[c]["out"], dtype=np.float32)
        for s in range(NQ):
            j = 2 * s + 1 if c % 2 == 0 else 2 * s
            full[b, 128 * j:128 * (j + 1), :] = o[s]
    return full


def kernel(**inputs):
    nc = build_program()
    in_maps = _stage_inputs(inputs)
    res = run_bass_kernel_spmd(nc, in_maps, core_ids=list(range(8)))
    return _assemble(res.results)


# version 07
# speedup vs baseline: 1.0403x; 1.0403x over previous
import numpy as np
import ml_dtypes
from contextlib import ExitStack

import concourse.bass as bass
import concourse.mybir as mybir
from concourse.bass_utils import run_bass_kernel_spmd

F32 = mybir.dt.float32
BF16 = mybir.dt.bfloat16
I32 = mybir.dt.int32
U8 = mybir.dt.uint8
AF = mybir.ActivationFunctionType
ALU = mybir.AluOpType
AX = mybir.AxisListType.X

D = 1024
HID = 3072
NH = 16
NT = 32
NQ = 16
S127 = float(np.float32(1.0 / 127.0))
EPS = 1e-5
KB = 1024

DEBUG = 0

ENGINES = ("sp", "pe", "act", "dve", "po")


class Sem:
    __slots__ = ("h", "n")

    def __init__(self, h):
        self.h = h
        self.n = 0


def build_program():
    nc = bass.Bass("TRN2", target_bir_lowering=False, debug=False)

    x_h = nc.dram_tensor("x", [NT, 128, D], F32, kind="ExternalInput").ap()
    wgT_h = nc.dram_tensor("wgT", [D, HID], I32, kind="ExternalInput").ap()
    wuT_h = nc.dram_tensor("wuT", [D, HID], I32, kind="ExternalInput").ap()
    wdT_h = nc.dram_tensor("wdT", [HID, D], I32, kind="ExternalInput").ap()
    wqkvT_h = nc.dram_tensor("wqkvT", [D, HID], F32, kind="ExternalInput").ap()
    woT_h = nc.dram_tensor("woT", [D, D], F32, kind="ExternalInput").ap()
    bqkvc_h = nc.dram_tensor("bqkvc", [128, 24], F32, kind="ExternalInput").ap()
    bvr_h = nc.dram_tensor("bvr", [1, D], F32, kind="ExternalInput").ap()
    bor_h = nc.dram_tensor("bor", [1, D], F32, kind="ExternalInput").ap()
    ghost_h = nc.dram_tensor("ghost", [1, NH], F32, kind="ExternalInput").ap()
    mask_h = nc.dram_tensor("mask", [128, 256], BF16, kind="ExternalInput").ap()
    eye_h = nc.dram_tensor("eye", [128, 128], BF16, kind="ExternalInput").ap()

    out_h = nc.dram_tensor("out", [NQ, 128, D], F32, kind="ExternalOutput").ap()

    spill_kind = "ExternalOutput" if DEBUG else "Internal"
    ln2_sp = nc.dram_tensor("ln2_sp", [NT, 128, D], BF16, kind=spill_kind).ap()
    h1_sp = nc.dram_tensor("h1_sp", [NQ, 128, D], F32, kind=spill_kind).ap()
    x2_sp = nc.dram_tensor("x2_sp", [NQ, 128, D], F32, kind=spill_kind).ap()
    ln3_sp = nc.dram_tensor("ln3_sp", [NQ, 128, D], BF16, kind=spill_kind).ap()

    with ExitStack() as stack:
        En = stack.enter_context

        AR = 201 * KB
        arena = En(nc.sbuf_tensor("arena", [128, AR], U8))[:, :]

        def carve(off_kb, shape, dtype, partitions=128):
            nb = int(np.prod(shape[1:])) * mybir.dt.size(dtype)
            off = (int(off_kb * KB) + 63) // 64 * 64
            assert off + nb <= AR, (off_kb, shape, dtype)
            ap = arena[:, off:off + nb].bitcast(dtype)
            if len(shape) == 3:
                ap = ap.rearrange("p (a b) -> p a b", a=shape[1])
            elif len(shape) == 4:
                ap = ap.rearrange("p (a b c) -> p a b c", a=shape[1], b=shape[2])
            if partitions != 128:
                ap = ap[0:partitions]
            return ap

        eye = En(nc.sbuf_tensor("eye_s", [128, 128], BF16))[:, :]
        mask = En(nc.sbuf_tensor("mask_s", [128, 256], BF16))[:, :]
        bqkvc = En(nc.sbuf_tensor("bqkvc_s", [128, 24], F32))[:, :]
        bq8c = En(nc.sbuf_tensor("bq8c", [128, 8], F32))[:, :]
        bvr = En(nc.sbuf_tensor("bvr_s", [1, D], BF16))[:, :]
        bor = En(nc.sbuf_tensor("bor_s", [1, D], BF16))[:, :]
        ghost = En(nc.sbuf_tensor("ghost_s", [1, NH], F32))[:, :]
        ones_bf = En(nc.sbuf_tensor("ones_bf", [1, 128], BF16))[:, :]
        onesf = En(nc.sbuf_tensor("onesf", [128, 64], F32))[:, :]
        stats = En(nc.sbuf_tensor("stats", [128, 32], F32))[:, :]
        tmpr = carve(154.56, [128, 512], F32)

        pb = [En(nc.psum_tensor(f"pb{i}", [128, 512], F32))[:, :] for i in range(7)]
        tpb = En(nc.psum_tensor("tpb", [128, 1024], BF16))[:, :]
        tpb3 = tpb.rearrange("p (a b) -> p a b", a=8)

        Q = {e: [] for e in ENGINES}

        def mksem(name):
            return Sem(En(nc.semaphore(name)))

        def op(eng, fn, sem=None, inc=1):
            if sem is None:
                Q[eng].append(fn)
                return None
            sem.n += inc
            v = sem.n
            Q[eng].append(lambda E, fn=fn, s=sem, i=inc: fn(E).then_inc(s.h, i))
            return v

        def wait(eng, sem, v):
            if v:
                Q[eng].append(lambda E, s=sem, v=v: E.wait_ge(s.h, v))

        bsems = {e: mksem(f"bar_{e}") for e in ENGINES}
        bar_scr = En(nc.sbuf_tensor("bar_scr", [1, 16], F32))[:, :]

        def barrier():
            vals = {}
            for e in ENGINES:
                if e == "dve":
                    vals[e] = op(e, lambda E: E.memset(bar_scr[0:1, 0:4], 0.0),
                                 sem=bsems[e])
                elif e == "act":
                    vals[e] = op(e, lambda E: E.activation(
                        out=bar_scr[0:1, 8:12], in_=bar_scr[0:1, 4:8],
                        func=AF.Identity), sem=bsems[e])
                else:
                    vals[e] = op(e, lambda E: E.nop(), sem=bsems[e])
            for e in ENGINES:
                for e2 in ENGINES:
                    if e2 != e:
                        wait(e, bsems[e2], vals[e2])

        s_small = mksem("s_small")
        sg_wg = mksem("sg_wg")
        sg_wu = mksem("sg_wu")
        sg_wd = mksem("sg_wd")
        sg_kvw = mksem("sg_kvw")
        sg_qw = mksem("sg_qw")
        sg_wo = mksem("sg_wo")
        s_x = [mksem(f"s_x{i}") for i in range(4)]
        s_h1w = [mksem(f"s_h1w{i}") for i in range(2)]
        s_ln2w = [mksem(f"s_ln2w{i}") for i in range(2)]
        s_ln2r = [mksem(f"s_ln2r{i}") for i in range(2)]
        s_lnqr = mksem("s_lnqr")
        s_h1r = [mksem(f"s_h1r{i}") for i in range(2)]
        s_x2w = [mksem(f"s_x2w{i}") for i in range(2)]
        s_ln3w = [mksem(f"s_ln3w{i}") for i in range(2)]
        s_ln3r = [mksem(f"s_ln3r{i}") for i in range(2)]
        s_x2r = [mksem(f"s_x2r{i}") for i in range(4)]
        s_out = [mksem(f"s_out{i}") for i in range(2)]
        pe = mksem("pe")
        act = mksem("act")
        dve = mksem("dve")

        op("sp", lambda E: E.dma_start(out=eye, in_=eye_h), sem=s_small, inc=16)
        op("sp", lambda E: E.dma_start(out=mask, in_=mask_h), sem=s_small, inc=16)
        op("sp", lambda E: E.dma_start(out=bqkvc, in_=bqkvc_h), sem=s_small, inc=16)
        op("sp", lambda E: E.dma_start(out=ghost, in_=ghost_h), sem=s_small, inc=16)
        op("po", lambda E: E.dma_start(out=bvr, in_=bvr_h), sem=s_small, inc=16)
        op("po", lambda E: E.dma_start(out=bor, in_=bor_h), sem=s_small, inc=16)
        v_small = s_small.n

        wgt = {"wg": carve(0, [128, 8, HID], BF16),
               "wu": carve(48, [128, 8, HID], BF16),
               "wd": carve(96, [128, 24, D], BF16)}
        op("po", lambda E, o=wgt["wg"]: E.dma_start(
            out=o, in_=wgT_h.rearrange("(c p) h -> p c h", p=128)), sem=sg_wg, inc=16)
        op("po", lambda E, o=wgt["wu"]: E.dma_start(
            out=o, in_=wuT_h.rearrange("(c p) h -> p c h", p=128)), sem=sg_wu, inc=16)
        op("po", lambda E, o=wgt["wd"]: E.dma_start(
            out=o, in_=wdT_h.rearrange("(c p) h -> p c h", p=128)), sem=sg_wd, inc=16)
        wg_load = [1]

        op("dve", lambda E: E.memset(ones_bf, 1.0))
        op("dve", lambda E: E.memset(onesf, 1.0))
        wait("dve", s_small, v_small)
        v_bq8 = op("dve", lambda E: E.tensor_scalar_mul(bq8c, bqkvc[:, 0:8], 0.125), sem=dve)

        def warmup(n=24):
            for _ in range(n):
                op("pe", lambda E: E.matmul(pb[0][:, 0:128], eye, eye,
                                            start=True, stop=True))

        ln_slot_free = [0, 0, 0, 0]
        ln_count = [0]
        sq_off = [188.0]

        def emit_ln(xa, out_ap, x_waits, out_war=()):
            r = ln_count[0] % 4
            ln_count[0] += 1
            st = stats[:, 8 * r:8 * r + 8]
            sums, nm, m2, ve, lnv, rstd, tmb = (st[:, i:i + 1] for i in range(7))
            sqs = st[:, 7:8]
            sq = carve(sq_off[0], [128, D], F32)
            for (s, v) in x_waits:
                wait("dve", s, v)
            wait("dve", act, ln_slot_free[r])
            v_sums = op("dve", lambda E, xa=xa, o=sums: E.reduce_sum(o, xa, axis=AX), sem=dve)
            for (s, v) in x_waits:
                wait("act", s, v)
            wait("act", act, ln_slot_free[r])
            v_sqs = op("act", lambda E, xa=xa, sq=sq, o=sqs: E.activation(
                sq, xa, AF.Square, accum_out=o[:, 0:1]), sem=act)
            wait("dve", dve, v_sums)
            wait("dve", act, v_sqs)
            v_nm = op("dve", lambda E, i=sums, o=nm: E.tensor_scalar_mul(o, i, -1.0 / D),
                      sem=dve)
            wait("dve", dve, v_nm)
            v_m2 = op("dve", lambda E, i=nm, o=m2: E.tensor_tensor(o, i, i, ALU.mult),
                      sem=dve)
            wait("dve", dve, v_m2)
            v_ve0 = op("dve", lambda E, i=sqs, m=m2, o=ve: E.scalar_tensor_tensor(
                o, i, 1.0 / D, m, ALU.mult, ALU.subtract), sem=dve)
            wait("dve", dve, v_ve0)
            v_ve = op("dve", lambda E, o=ve: E.tensor_scalar_add(o, o, EPS), sem=dve)
            wait("act", dve, v_ve)
            v_sr = op("act", lambda E, i=ve, o=lnv: E.activation(
                o, i, AF.Sqrt), sem=act)
            wait("dve", act, v_sr)
            v_rr = op("dve", lambda E, i=lnv, o=rstd: E.reciprocal(o, i), sem=dve)
            wait("dve", dve, v_rr)
            v_tmb = op("dve", lambda E, a=nm, b=rstd, o=tmb: E.tensor_tensor(
                o, a, b, ALU.mult), sem=dve)
            wait("act", dve, v_tmb)
            for (s_, v_) in out_war:
                wait("act", s_, v_)
            v_out = op("act", lambda E, xa=xa, o=out_ap, sc=rstd, bi=tmb: E.activation(
                o, xa, AF.Identity, bias=bi[:, 0:1], scale=sc[:, 0:1]), sem=act)
            ln_slot_free[r] = v_out
            return v_out

        tp_free = [0]

        def emit_transpose(ln_ap, lnT_dst, v_ready, rsem):
            wait("pe", rsem, v_ready) if rsem is not None else None
            wait("pe", dve, tp_free[0])
            v_tp = None
            for j in range(8):
                v_tp = op("pe", lambda E, o=tpb[:, 128 * j:128 * (j + 1)],
                          i=ln_ap[:, 128 * j:128 * (j + 1)]: E.transpose(o, i, eye),
                          sem=pe if j == 7 else None)
            wait("dve", pe, v_tp)
            v_ev = op("dve", lambda E, d=lnT_dst: E.tensor_copy(d, tpb3), sem=dve)
            tp_free[0] = v_ev
            return v_ev, v_tp

        def emit_ffn(lnT, v_lnT_ready, hp_off, sg_off, sink, war_free, wg=None):
            w_ = wgt
            hp = carve(hp_off, [128, 12, 256], BF16)
            sgb = [carve(sg_off + 0.5 * i, [128, 256], BF16) for i in range(2)]
            usb = [carve(sg_off + 1.0 + 0.5 * i, [128, 256], BF16) for i in range(2)]
            wait("pe", dve, v_lnT_ready)
            wait("pe", sg_wg, 16 * wg_load[0])
            wait("pe", sg_wu, 16 * wg_load[0])
            wait("pe", sg_wd, 16 * wg_load[0])
            v_hp = {}
            v_evac = {}
            last_down_v = 0
            for half in range(2):
                for k in range(12):
                    ht = 12 * half + k
                    if ht > 0:
                        wait("pe", act, v_evac[ht - 1][0])
                    v_g = None
                    for c in range(8):
                        v_g = op("pe", lambda E, o=w_["wg"][:, c, 128 * ht:128 * (ht + 1)],
                                 r=lnT[:, c, :], st=(c == 0), sp=(c == 7):
                                 E.matmul(pb[0][:, 0:256], o, r, start=st, stop=sp),
                                 sem=pe if c == 7 else None)
                    if ht > 0:
                        wait("pe", act, v_evac[ht - 1][1])
                    v_u = None
                    for c in range(8):
                        v_u = op("pe", lambda E, o=w_["wu"][:, c, 128 * ht:128 * (ht + 1)],
                                 r=lnT[:, c, :], st=(c == 0), sp=(c == 7):
                                 E.matmul(pb[1][:, 0:256], o, r, start=st, stop=sp),
                                 sem=pe if c == 7 else None)
                    wait("act", pe, v_g)
                    if ht >= 2:
                        wait("act", dve, v_hp[ht - 2])
                    v_sg = op("act", lambda E, o=sgb[ht % 2]: E.activation(
                        o, pb[0][:, 0:256], AF.Silu, scale=S127), sem=act)
                    wait("act", pe, v_u)
                    v_us = op("act", lambda E, o=usb[ht % 2]: E.activation(
                        o, pb[1][:, 0:256], AF.Identity, scale=0.5 * S127 * S127), sem=act)
                    v_evac[ht] = (v_sg, v_us)
                    wait("dve", act, v_us)
                    if half == 1:
                        wait("dve", pe, last_down_v)
                    v_hp[ht] = op("dve", lambda E, o=hp[:, ht % 12, :], a=sgb[ht % 2],
                                  b=usb[ht % 2]: E.tensor_tensor(o, a, b, ALU.mult), sem=dve)
                for k in range(12):
                    ht = 12 * half + k
                    wait("pe", dve, v_hp[ht])
                    if ht == 0:
                        for v in war_free:
                            wait("pe", dve, v)
                    st_ = (ht == 0)
                    sp_ = (ht == 23)
                    for tt in range(2):
                        lo, hi = [(pb[4], pb[5]), (pb[2], pb[3])][tt]
                        op("pe", lambda E, l=hp[:, ht % 12, 128 * tt:128 * (tt + 1)],
                           r=w_["wd"][:, ht, 0:512], o=lo, st=st_, sp=sp_:
                           E.matmul(o[:, 0:512], l, r, start=st, stop=sp))
                        last_down_v = op("pe", lambda E, l=hp[:, ht % 12, 128 * tt:128 * (tt + 1)],
                                         r=w_["wd"][:, ht, 512:1024], o=hi, st=st_, sp=sp_:
                                         E.matmul(o[:, 0:512], l, r, start=st, stop=sp),
                                         sem=pe)
            frees = []
            for tt in range(2):
                lo, hi = [(pb[4], pb[5]), (pb[2], pb[3])][tt]
                frees.append(sink(tt, lo, hi, last_down_v))
            return frees

        x_ring = [carve(144 + 4 * i, [128, D], F32) for i in range(4)]
        ln1_ring = [carve(193 + 2 * i, [128, D], BF16) for i in range(4)]
        lnT_ring = [carve(160 + 4 * i, [128, 8, 256], BF16) for i in range(2)]
        ln2_ring = [carve(176 + 2 * i, [128, D], BF16) for i in range(2)]
        h1_ring = [carve(180 + 4 * i, [128, D], F32) for i in range(2)]

        ms_h1add = {}
        ms_ln2 = {}
        ms_h1w = {}
        ms_ln2w = {}
        xld = {}
        v_ln1s = {}
        v_tp1 = {}
        war_free = []

        def p1_tile_head(t):
            if t >= 4:
                wait("sp", dve, ms_h1add[t - 4])
            xld[t] = op("sp", lambda E, o=x_ring[t % 4], i=x_h[t]: E.dma_start(out=o, in_=i),
                        sem=s_x[t % 4], inc=16)
            v_ln1s[t] = emit_ln(x_ring[t % 4], ln1_ring[t % 4], [(s_x[t % 4], xld[t])],
                                out_war=([(pe, v_tp1[t - 4])] if t >= 4 else ()))

        wait("pe", s_small, v_small)
        for t in (0, 1):
            p1_tile_head(t)
        for w in range(NT // 2):
            for t in (2 * w + 2, 2 * w + 3):
                if t < NT:
                    p1_tile_head(t)
            v_lnT_w = None
            for t in (2 * w, 2 * w + 1):
                v_lnT_w, v_tp1[t] = emit_transpose(
                    ln1_ring[t % 4],
                    lnT_ring[w % 2][:, :, 128 * (t % 2):128 * (t % 2 + 1)], v_ln1s[t], act)
            if w == 0:
                warmup()
            t0 = 2 * w

            def sink(tt, lo, hi, v_pe, t0=t0):
                tl = t0 + tt
                wait("dve", pe, v_pe)
                wait("dve", s_x[tl % 4], xld[tl])
                if tl >= 2:
                    wait("dve", act, ms_ln2[tl - 2])
                    if (tl - 2) % 2 == 1:
                        wait("dve", s_h1w[((tl - 2) // 2) % 2], ms_h1w[tl - 2])
                op("dve", lambda E, o=h1_ring[tl % 2][:, 0:512], a=lo[:, 0:512],
                   b=x_ring[tl % 4][:, 0:512]: E.tensor_tensor(o, a, b, ALU.add))
                v_h1 = op("dve", lambda E, o=h1_ring[tl % 2][:, 512:1024], a=hi[:, 0:512],
                          b=x_ring[tl % 4][:, 512:1024]: E.tensor_tensor(o, a, b, ALU.add),
                          sem=dve)
                ms_h1add[tl] = v_h1
                if tl % 2 == 1:
                    si = tl // 2
                    wait("sp", dve, v_h1)
                    if si >= 2:
                        wait("sp", s_h1w[si % 2], 16 * (si // 2))
                    ms_h1w[tl] = op("sp", lambda E, o=h1_sp[si], i=h1_ring[tl % 2]:
                                    E.dma_start(out=o, in_=i), sem=s_h1w[si % 2], inc=16)
                v_ln2 = emit_ln(h1_ring[tl % 2], ln2_ring[tl % 2], [(dve, v_h1)],
                                out_war=([(s_ln2w[tl % 2], ms_ln2w[tl - 2])]
                                         if tl >= 2 else ()))
                ms_ln2[tl] = v_ln2
                wait("sp", act, v_ln2)
                ms_ln2w[tl] = op("sp", lambda E, o=ln2_sp[tl], i=ln2_ring[tl % 2]:
                                 E.dma_start(out=o, in_=i), sem=s_ln2w[tl % 2], inc=16)
                return v_h1

            war_free = emit_ffn(lnT_ring[w % 2], v_lnT_w, 170.0, 168.0, sink, war_free)

        barrier()
        ln2w_tot = (s_ln2w[0].n, s_ln2w[1].n)
        h1w_tot = (s_h1w[0].n, s_h1w[1].n)

        K_sb = carve(0, [128, 8, 4096], BF16)
        V_sb = carve(64, [128, NT, NH, 65], BF16)
        kvw = carve(130.56, [128, 8, 2048], BF16)
        ln2w_ring = [carve(162.56 + 8 * i, [128, 4, D], BF16) for i in range(2)]
        ln2T_ring = [carve(178.56 + 8 * i, [128, 8, 512], BF16) for i in range(2)]

        op("po", lambda E: E.dma_start(
            out=kvw, in_=wqkvT_h[:, 1024:3072].rearrange("(c p) h -> p c h", p=128)),
            sem=sg_kvw, inc=16)
        v_vones = op("dve", lambda E: E.memset(V_sb[:, :, :, 64:65], 1.0), sem=dve)

        wait("sp", s_ln2w[0], ln2w_tot[0])
        wait("sp", s_ln2w[1], ln2w_tot[1])
        warmup()
        ms_kevac = {}
        ms_vevac = {}
        v_tr_last = {}
        for w in range(8):
            if w >= 2:
                wait("sp", pe, v_tr_last[w - 2])
            ld2 = op("sp", lambda E, o=ln2w_ring[w % 2],
                     i=ln2_sp[4 * w:4 * w + 4].rearrange("t p f -> p t f"):
                     E.dma_start(out=o, in_=i), sem=s_ln2r[w % 2], inc=16)
            v_lnT_w = None
            for tl in range(4):
                _, v_tp = emit_transpose(
                    ln2w_ring[w % 2][:, tl, :],
                    ln2T_ring[w % 2][:, :, 128 * tl:128 * (tl + 1)], ld2, s_ln2r[w % 2])
                v_tr_last[w] = v_tp
                v_lnT_w = tp_free[0]
            wait("pe", dve, v_lnT_w)
            wait("pe", sg_kvw, 16)
            wait("act", s_small, v_small)
            for j in range(8):
                bk = pb[j % 3]
                if j >= 3:
                    wait("pe", act, ms_kevac[(w, j - 3)])
                elif w > 0:
                    wait("pe", act, ms_kevac[(w - 1, j + 6 if j <= 1 else 5)])
                v_k = None
                for c in range(8):
                    v_k = op("pe", lambda E, o=bk[:, 0:512], l=kvw[:, c, 128 * j:128 * (j + 1)],
                             r=ln2T_ring[w % 2][:, c, :], st=(c == 0), sp=(c == 7):
                             E.matmul(o, l, r, start=st, stop=sp), sem=pe if c == 7 else None)
                wait("act", pe, v_k)
                ms_kevac[(w, j)] = op("act", lambda E, o=K_sb[:, j, 512 * w:512 * (w + 1)],
                                      i=bk[:, 0:512], b=bqkvc[:, 8 + j:9 + j]:
                                      E.activation(o, i, AF.Identity, bias=b[:, 0:1]),
                                      sem=act)
            for tl in range(4):
                t = 4 * w + tl
                if t >= 1:
                    wait("pe", act, ms_vevac[t - 1])
                wait("pe", s_small, v_small)
                v_v = None
                for c in range(8):
                    for hh in range(2):
                        op("pe", lambda E, o=pb[4 + hh][:, 0:512],
                           l=ln2T_ring[w % 2][:, c, 128 * tl:128 * (tl + 1)],
                           r=kvw[:, c, 1024 + 512 * hh:1024 + 512 * (hh + 1)], st=(c == 0):
                           E.matmul(o, l, r, start=st, stop=False))
                for hh in range(2):
                    v_v = op("pe", lambda E, o=pb[4 + hh][:, 0:512], l=ones_bf[0:1, 0:128],
                             r=bvr[0:1, 512 * hh:512 * (hh + 1)]:
                             E.matmul(o, l, r, start=False, stop=True),
                             sem=pe if hh == 1 else None)
                wait("act", pe, v_v)
                wait("act", dve, v_vones)
                op("act", lambda E, o=V_sb[:, t, 0:8, 0:64],
                   i=pb[4][:, 0:512].rearrange("p (a b) -> p a b", a=8):
                   E.activation(o, i, AF.Identity))
                ms_vevac[t] = op("act", lambda E, o=V_sb[:, t, 8:16, 0:64],
                                 i=pb[5][:, 0:512].rearrange("p (a b) -> p a b", a=8):
                                 E.activation(o, i, AF.Identity), sem=act)

        barrier()

        QA = carve(162.56, [128, NQ, 8, 128], BF16)
        qw = carve(130.56, [128, 8, 1024], BF16)
        lnq_ring = carve(146.56, [128, 4, D], BF16)
        lnqT = carve(154.56, [128, 8, 512], BF16)
        op("po", lambda E: E.dma_start(
            out=qw, in_=wqkvT_h[:, 0:1024].rearrange("(c p) h -> p c h", p=128)),
            sem=sg_qw, inc=16)
        ln2_odd = ln2_sp.rearrange("(a two) p f -> a two p f", two=2)
        warmup()
        ms_qevac = {}
        v_trq_last = 0
        for w in range(4):
            if w >= 1:
                wait("sp", pe, v_trq_last)
            ldq = op("sp", lambda E, o=lnq_ring,
                     i=ln2_odd[4 * w:4 * w + 4, 1:2].rearrange("t one p f -> p (t one) f"):
                     E.dma_start(out=o, in_=i), sem=s_lnqr, inc=16)
            v_lnT_w = None
            for tl in range(4):
                _, v_tp = emit_transpose(lnq_ring[:, tl, :],
                                         lnqT[:, :, 128 * tl:128 * (tl + 1)], ldq, s_lnqr)
                v_trq_last = v_tp
                v_lnT_w = tp_free[0]
            wait("pe", dve, v_lnT_w)
            wait("pe", sg_qw, 16)
            wait("act", dve, v_bq8)
            for j in range(8):
                bk = pb[j % 3]
                if j >= 3:
                    wait("pe", act, ms_qevac[(w, j - 3)])
                elif w > 0:
                    wait("pe", act, ms_qevac[(w - 1, j + 6 if j <= 1 else 5)])
                v_q = None
                for c in range(8):
                    v_q = op("pe", lambda E, o=bk[:, 0:512], l=qw[:, c, 128 * j:128 * (j + 1)],
                             r=lnqT[:, c, :], st=(c == 0), sp=(c == 7):
                             E.matmul(o, l, r, start=st, stop=sp), sem=pe if c == 7 else None)
                wait("act", pe, v_q)
                ms_qevac[(w, j)] = op("act", lambda E, o=QA[:, 4 * w:4 * w + 4, j, :],
                                      i=bk[:, 0:512].rearrange("p (a b) -> p a b", a=4),
                                      b=bq8c[:, j:j + 1]:
                                      E.activation(o, i, AF.Identity, bias=b[:, 0:1],
                                                   scale=0.125), sem=act)

        barrier()

        wo = carve(130.56, [128, 8, D], BF16)
        zbuf = [carve(146.56 + 1.0 * i, [128, 512], BF16) for i in range(4)]
        rbs_ring = [carve(150.56 + 2.0 * i, [128, 512], F32, partitions=64) for i in range(2)]
        op("po", lambda E: E.dma_start(
            out=wo, in_=woT_h.rearrange("(c p) d -> p c d", p=128)), sem=sg_wo, inc=16)

        st_lo = [pb[0], pb[1]]
        st_hi = [pb[6], pb[2]]
        a_banks = [pb[3], pb[4]]
        rbb = pb[5]
        st_rot = [0]
        z_rot = [0]
        ms_exp = {}
        ms_zcons = {}
        ms_afree = [0, 0]
        ms_rbfree = [0]

        wait("pe", s_small, v_small)
        wait("dve", s_small, v_small)
        warmup()

        def emit_qk(p, g, t, n256):
            j = g
            s0 = 2 * p
            si = st_rot[0]
            st_rot[0] += 1
            stl = st_lo[si % 2]
            sth = st_hi[si % 2]
            zi = z_rot[0]
            z_rot[0] += 1
            zb = zbuf[zi % 4]
            if si >= 2:
                wait("pe", act, ms_exp[si - 2])
            v_st = None
            for hh in range(2):
                base = 64 * hh
                if n256:
                    qs0, qs1 = s0, s0 + 2
                    ncol = 256
                else:
                    qs0, qs1 = s0 + 1, s0 + 2
                    ncol = 128
                v_st = op("pe", lambda E, o=(stl if hh == 0 else sth)[:, 0:ncol],
                          l=K_sb[base:base + 64, j, 128 * t:128 * (t + 1)],
                          r=QA[base:base + 64, qs0:qs1, j, :]:
                          E.matmul(o, l, r, start=True, stop=True),
                          sem=pe if hh == 1 else None)
            wait("act", pe, v_st)
            if zi >= 4:
                wait("act", pe, ms_zcons[zi - 4])
            half = 256 if n256 else 128
            op("act", lambda E, o=zb[:, 0:half], i=stl[:, 0:half]:
               E.activation(o, i, AF.Exp))
            v_z = op("act", lambda E, o=zb[:, half:2 * half], i=sth[:, 0:half]:
                     E.activation(o, i, AF.Exp), sem=act)
            ms_exp[si] = v_z
            gate_sem, gate_v = act, v_z
            masked = (t >= 4 * p)
            if masked:
                if n256:
                    mc = mask[:, 0:128] if t == 4 * p else mask[:, 128:256]
                    cols = [(0, 128), (256, 384)]
                else:
                    mc = mask[:, 0:128] if t == 4 * p + 2 else mask[:, 128:256]
                    cols = [(0, 128), (128, 256)]
                wait("dve", act, v_z)
                v_m = None
                for (c0, c1) in cols:
                    v_m = op("dve", lambda E, o=zb[:, c0:c1], m=mc:
                             E.tensor_tensor(o, o, m, ALU.mult),
                             sem=dve if (c0, c1) == cols[-1] else None)
                gate_sem, gate_v = dve, v_m
            return (zi, zb, gate_sem, gate_v, n256)

        def emit_sv(p, g, t, gate, first, last):
            zi, zb, gate_sem, gate_v, n256 = gate
            abank = a_banks[g % 2]
            wait("pe", gate_sem, gate_v)
            if first:
                wait("pe", dve, ms_afree[g % 2])
            v_sv = None
            for hh in range(2):
                h = 2 * g + hh
                if n256:
                    ac, zc, ncol = 256 * hh, 256 * hh, 256
                else:
                    ac, zc, ncol = 256 * hh + 128, 128 * hh, 128
                v_sv = op("pe", lambda E, o=abank[0:65, ac:ac + ncol],
                          l=V_sb[:, t, h, 0:65], r=zb[:, zc:zc + ncol],
                          st=(first and hh == 0), sp=last:
                          E.matmul(o, l, r, start=st, stop=sp),
                          sem=pe if hh == 1 else None)
            ms_zcons[zi] = v_sv
            return v_sv

        tmpb = carve(156.56, [1, 512], BF16, partitions=1)

        def emit_evac(p, g, v_sv):
            abank = a_banks[g % 2]
            ri = g % 2
            wait("dve", pe, v_sv)
            v_gh = None
            for hh in range(2):
                h = 2 * g + hh
                v_gh = op("dve", lambda E, o=tmpr[0:1, 256 * hh:256 * hh + 256],
                          i=abank[64:65, 256 * hh:256 * hh + 256], gh=ghost[0:1, h:h + 1]:
                          E.tensor_scalar(o, i, gh, None, ALU.add),
                          sem=dve if hh == 1 else None)
            wait("dve", dve, v_gh)

            def _recip_bf16(E):
                with nc.allow_low_precision(reason="normalizer feeds bf16 operand"):
                    return E.reciprocal(tmpb[0:1, 0:512], tmpr[0:1, 0:512])

            v_rr = op("dve", _recip_bf16, sem=dve)
            wait("pe", dve, v_rr)
            wait("pe", dve, ms_rbfree[0])
            v_rb = op("pe", lambda E, o=rbb[0:64, 0:512]: E.matmul(
                o, ones_bf[0:1, 0:64], tmpb[0:1, 0:512], start=True, stop=True), sem=pe)
            wait("dve", pe, v_rb)
            v_c = op("dve", lambda E, o=rbs_ring[ri], i=rbb[0:64, 0:512]:
                     E.tensor_copy(o, i), sem=dve)
            s0 = 2 * p
            v_a = None
            for hh in range(2):
                dst = QA[64 * hh:64 * hh + 64, s0:s0 + 2, g, :]
                v_a = op("dve", lambda E, o=dst,
                         a=abank[0:64, 256 * hh:256 * hh + 256].rearrange(
                             "p (a b) -> p a b", a=2),
                         b=rbs_ring[ri][0:64, 256 * hh:256 * hh + 256].rearrange(
                             "p (a b) -> p a b", a=2):
                         E.tensor_tensor(o, a, b, ALU.mult), sem=dve)
            ms_afree[g % 2] = v_a
            ms_rbfree[0] = v_c

        pairs = [(p, a) for p in range(8) for a in range(4)]
        pending = None
        for p, a in pairs:
            T = 4 * p + 4
            visits = [(t, t < 4 * p + 2) for t in range(T)]
            g0, g1 = 2 * a, 2 * a + 1
            gates0 = {0: emit_qk(p, g0, *visits[0])}
            gates1 = {0: emit_qk(p, g1, *visits[0])}
            if pending is not None:
                emit_evac(*pending[0])
                emit_evac(*pending[1])
                pending = None
            v_sv0 = v_sv1 = None
            for i in range(T):
                if i + 1 < T:
                    gates0[i + 1] = emit_qk(p, g0, *visits[i + 1])
                    gates1[i + 1] = emit_qk(p, g1, *visits[i + 1])
                t, _ = visits[i]
                v_sv0 = emit_sv(p, g0, t, gates0.pop(i), i == 0, i == T - 1)
                v_sv1 = emit_sv(p, g1, t, gates1.pop(i), i == 0, i == T - 1)
            pending = ((p, g0, v_sv0), (p, g1, v_sv1))
        emit_evac(*pending[0])
        emit_evac(*pending[1])

        barrier()

        h1in_ring = [carve(0 + 4 * i, [128, D], F32) for i in range(2)]
        x2_ring = [carve(8 + 4 * i, [128, D], F32) for i in range(2)]
        ln3_ring = [carve(146.56 + 2 * i, [128, D], BF16) for i in range(2)]
        sq_off[0] = 150.56
        wgt2 = {"wg": carve(16, [128, 8, HID], BF16),
                "wu": carve(64, [128, 8, HID], BF16)}
        op("po", lambda E, o=wgt2["wg"]: E.dma_start(
            out=o, in_=wgT_h.rearrange("(c p) h -> p c h", p=128)), sem=sg_wg, inc=16)
        op("po", lambda E, o=wgt2["wu"]: E.dma_start(
            out=o, in_=wuT_h.rearrange("(c p) h -> p c h", p=128)), sem=sg_wu, inc=16)

        ms_x2 = {}
        ms_ln3 = {}
        ms_ln3w = {}
        wait("pe", sg_wo, 16)
        warmup()
        for s in range(NQ):
            if s >= 2:
                wait("sp", dve, ms_x2[s - 2])
                wait("sp", s_h1r[s % 2], 16 * (s // 2))
            if s == 0:
                wait("sp", s_h1w[0], h1w_tot[0])
                wait("sp", s_h1w[1], h1w_tot[1])
            ldh = op("sp", lambda E, o=h1in_ring[s % 2], i=h1_sp[s]: E.dma_start(out=o, in_=i),
                     sem=s_h1r[s % 2], inc=16)
            if s >= 1:
                wait("pe", dve, ms_x2[s - 1])
            v_wo = None
            for half in range(2):
                bank = pb[4 + half]
                for c in range(8):
                    op("pe", lambda E, o=bank[:, 0:512], l=QA[:, s, c, :],
                       r=wo[:, c, 512 * half:512 * (half + 1)], st=(c == 0):
                       E.matmul(o, l, r, start=st, stop=False))
                v_wo = op("pe", lambda E, o=bank[:, 0:512], l=ones_bf[0:1, 0:128],
                          r=bor[0:1, 512 * half:512 * (half + 1)]:
                          E.matmul(o, l, r, start=False, stop=True), sem=pe)
            wait("dve", pe, v_wo)
            wait("dve", s_h1r[s % 2], ldh)
            if s >= 2:
                wait("dve", act, ms_ln3[s - 2])
                wait("dve", s_x2w[s % 2], 16 * (s // 2))
            op("dve", lambda E, o=x2_ring[s % 2][:, 0:512], a=pb[4][:, 0:512],
               b=h1in_ring[s % 2][:, 0:512]: E.tensor_tensor(o, a, b, ALU.add))
            v_x2 = op("dve", lambda E, o=x2_ring[s % 2][:, 512:1024], a=pb[5][:, 0:512],
                      b=h1in_ring[s % 2][:, 512:1024]: E.tensor_tensor(o, a, b, ALU.add),
                      sem=dve)
            ms_x2[s] = v_x2
            wait("sp", dve, v_x2)
            op("sp", lambda E, o=x2_sp[s], i=x2_ring[s % 2]: E.dma_start(out=o, in_=i),
               sem=s_x2w[s % 2], inc=16)
            v_ln3 = emit_ln(x2_ring[s % 2], ln3_ring[s % 2], [(dve, v_x2)],
                            out_war=([(s_ln3w[s % 2], ms_ln3w[s - 2])] if s >= 2 else ()))
            ms_ln3[s] = v_ln3
            wait("sp", act, v_ln3)
            ms_ln3w[s] = op("sp", lambda E, o=ln3_sp[s], i=ln3_ring[s % 2]:
                            E.dma_start(out=o, in_=i), sem=s_ln3w[s % 2], inc=16)

        barrier()
        x2w_tot = (s_x2w[0].n, s_x2w[1].n)
        ln3w_tot = (s_ln3w[0].n, s_ln3w[1].n)

        wgt2["wd"] = carve(112, [128, 24, D], BF16)
        op("po", lambda E, o=wgt2["wd"]: E.dma_start(
            out=o, in_=wdT_h.rearrange("(c p) h -> p c h", p=128)), sem=sg_wd, inc=16)
        wgt = wgt2
        wg_load[0] = 2
        x2in_ring = [carve(0 + 4 * i, [128, D], F32) for i in range(4)]
        out_ring = [carve(160 + 4 * i, [128, D], F32) for i in range(2)]
        ln3in_ring = [carve(168 + 4 * i, [128, 2, D], BF16) for i in range(2)]
        lnT5_ring = [carve(176 + 4 * i, [128, 8, 256], BF16) for i in range(2)]

        war_free = []
        ms_out = {}
        v_tr5 = {}
        xld5 = {}
        warmup()
        for w in range(8):
            t0 = 2 * w
            if w == 0:
                wait("sp", s_ln3w[0], ln3w_tot[0])
                wait("sp", s_ln3w[1], ln3w_tot[1])
                wait("sp", s_x2w[0], x2w_tot[0])
                wait("sp", s_x2w[1], x2w_tot[1])
            if w >= 2:
                wait("sp", pe, v_tr5[w - 2])
                wait("sp", s_ln3r[w % 2], 16 * (w // 2))
            ldl = op("sp", lambda E, o=ln3in_ring[w % 2],
                     i=ln3_sp[t0:t0 + 2].rearrange("t p f -> p t f"):
                     E.dma_start(out=o, in_=i), sem=s_ln3r[w % 2], inc=16)
            for tt in range(2):
                s = t0 + tt
                if s >= 4:
                    wait("sp", dve, ms_out[s - 4])
                    wait("sp", s_x2r[s % 4], 16 * (s // 4))
                xld5[s] = op("sp", lambda E, o=x2in_ring[s % 4], i=x2_sp[s]:
                             E.dma_start(out=o, in_=i), sem=s_x2r[s % 4], inc=16)
            v_lnT_w = None
            for tt in range(2):
                _, v_tp = emit_transpose(ln3in_ring[w % 2][:, tt, :],
                                         lnT5_ring[w % 2][:, :, 128 * tt:128 * (tt + 1)],
                                         ldl, s_ln3r[w % 2])
                v_tr5[w] = v_tp
                v_lnT_w = tp_free[0]

            def sink5(tt, lo, hi, v_pe, t0=t0):
                s = t0 + tt
                wait("dve", pe, v_pe)
                wait("dve", s_x2r[s % 4], xld5[s])
                if s >= 2:
                    wait("dve", s_out[s % 2], 16 * (s // 2))
                op("dve", lambda E, o=out_ring[s % 2][:, 0:512], a=lo[:, 0:512],
                   b=x2in_ring[s % 4][:, 0:512]: E.tensor_tensor(o, a, b, ALU.add))
                v_o = op("dve", lambda E, o=out_ring[s % 2][:, 512:1024], a=hi[:, 0:512],
                         b=x2in_ring[s % 4][:, 512:1024]: E.tensor_tensor(o, a, b, ALU.add),
                         sem=dve)
                ms_out[s] = v_o
                wait("sp", dve, v_o)
                op("sp", lambda E, o=out_h[s], i=out_ring[s % 2]:
                   E.dma_start(out=o, in_=i), sem=s_out[s % 2], inc=16)
                return v_o

            war_free = emit_ffn(lnT5_ring[w % 2], v_lnT_w, 186.0, 184.0, sink5, war_free)

        wait("sp", s_out[0], s_out[0].n)
        wait("sp", s_out[1], s_out[1].n)

        with nc.Block() as block:
            block.sync(lambda e, ops=Q["sp"]: [f(e) for f in ops])
            block.tensor(lambda e, ops=Q["pe"]: [f(e) for f in ops])
            block.scalar(lambda e, ops=Q["act"]: [f(e) for f in ops])
            block.vector(lambda e, ops=Q["dve"]: [f(e) for f in ops])
            block.gpsimd(lambda e, ops=Q["po"]: [f(e) for f in ops])

    return nc


def _stage_inputs(inputs):
    x = np.asarray(inputs["L0_attn_inputs"], dtype=np.float32)
    wqkv = np.asarray(inputs["L0_attn_Wqkv_w"], dtype=np.float32)
    bqkv = np.asarray(inputs["L0_attn_Wqkv_b"], dtype=np.float32)
    wo_w = np.asarray(inputs["L0_attn_Wo_w"], dtype=np.float32)
    bo = np.asarray(inputs["L0_attn_Wo_b"], dtype=np.float32)
    ghost = np.asarray(inputs["L0_attn_ghost"], dtype=np.float32)
    wg = np.asarray(inputs["L0_ffn_w_gate"], dtype=np.int32)
    wu = np.asarray(inputs["L0_ffn_w_up"], dtype=np.int32)
    wdn = np.asarray(inputs["L0_ffn_w_down"], dtype=np.int32)

    shared = {
        "wgT": np.ascontiguousarray(wg.T),
        "wuT": np.ascontiguousarray(wu.T),
        "wdT": np.ascontiguousarray(wdn.T),
        "wqkvT": np.ascontiguousarray(wqkv.T),
        "woT": np.ascontiguousarray(wo_w.T),
        "bqkvc": np.ascontiguousarray(bqkv.reshape(24, 128).T),
        "bvr": np.ascontiguousarray(bqkv[2048:].reshape(1, D)),
        "bor": np.ascontiguousarray(bo.reshape(1, D)),
        "ghost": np.ascontiguousarray(ghost.reshape(1, NH)),
        "eye": np.eye(128, dtype=ml_dtypes.bfloat16),
    }
    tri = np.triu(np.ones((128, 128), dtype=np.float32)).astype(ml_dtypes.bfloat16)
    ones = np.ones((128, 128), dtype=ml_dtypes.bfloat16)
    zeros = np.zeros((128, 128), dtype=ml_dtypes.bfloat16)
    mask_u = np.ascontiguousarray(np.concatenate([ones, tri], axis=1))
    mask_v = np.ascontiguousarray(np.concatenate([zeros, tri], axis=1))

    perm_v = np.arange(NT)
    perm_v[0::2] += 1
    perm_v[1::2] -= 1

    in_maps = []
    for c in range(8):
        b = c // 2
        xt = x[b].reshape(NT, 128, D)
        if c % 2 == 1:
            xt = xt[perm_v]
        m = dict(shared)
        m["x"] = np.ascontiguousarray(xt)
        m["mask"] = mask_u if c % 2 == 0 else mask_v
        in_maps.append(m)
    return in_maps


def _assemble(results):
    full = np.empty((4, NT * 128, D), dtype=np.float32)
    for c in range(8):
        b = c // 2
        o = np.asarray(results[c]["out"], dtype=np.float32)
        for s in range(NQ):
            j = 2 * s + 1 if c % 2 == 0 else 2 * s
            full[b, 128 * j:128 * (j + 1), :] = o[s]
    return full


def kernel(**inputs):
    nc = build_program()
    in_maps = _stage_inputs(inputs)
    res = run_bass_kernel_spmd(nc, in_maps, core_ids=list(range(8)))
    return _assemble(res.results)


# version 08
# speedup vs baseline: 1.0444x; 1.0039x over previous
import numpy as np
import ml_dtypes
from contextlib import ExitStack

import concourse.bass as bass
import concourse.mybir as mybir
from concourse.bass_utils import run_bass_kernel_spmd

F32 = mybir.dt.float32
BF16 = mybir.dt.bfloat16
I32 = mybir.dt.int32
U8 = mybir.dt.uint8
AF = mybir.ActivationFunctionType
ALU = mybir.AluOpType
AX = mybir.AxisListType.X

D = 1024
HID = 3072
NH = 16
NT = 32
NQ = 16
S127 = float(np.float32(1.0 / 127.0))
EPS = 1e-5
KB = 1024

DEBUG = 0

ENGINES = ("sp", "pe", "act", "dve", "po")


class Sem:
    __slots__ = ("h", "n")

    def __init__(self, h):
        self.h = h
        self.n = 0


def build_program():
    nc = bass.Bass("TRN2", target_bir_lowering=False, debug=False)

    x_h = nc.dram_tensor("x", [NT, 128, D], F32, kind="ExternalInput").ap()
    wgT_h = nc.dram_tensor("wgT", [D, HID], I32, kind="ExternalInput").ap()
    wuT_h = nc.dram_tensor("wuT", [D, HID], I32, kind="ExternalInput").ap()
    wdT_h = nc.dram_tensor("wdT", [HID, D], I32, kind="ExternalInput").ap()
    wqkvT_h = nc.dram_tensor("wqkvT", [D, HID], F32, kind="ExternalInput").ap()
    woT_h = nc.dram_tensor("woT", [D, D], F32, kind="ExternalInput").ap()
    bqkvc_h = nc.dram_tensor("bqkvc", [128, 24], F32, kind="ExternalInput").ap()
    bvr_h = nc.dram_tensor("bvr", [1, D], F32, kind="ExternalInput").ap()
    bor_h = nc.dram_tensor("bor", [1, D], F32, kind="ExternalInput").ap()
    ghost_h = nc.dram_tensor("ghost", [1, NH], F32, kind="ExternalInput").ap()
    mask_h = nc.dram_tensor("mask", [128, 256], BF16, kind="ExternalInput").ap()
    eye_h = nc.dram_tensor("eye", [128, 128], BF16, kind="ExternalInput").ap()

    out_h = nc.dram_tensor("out", [NQ, 128, D], F32, kind="ExternalOutput").ap()

    spill_kind = "ExternalOutput" if DEBUG else "Internal"
    ln2_sp = nc.dram_tensor("ln2_sp", [NT, 128, D], BF16, kind=spill_kind).ap()
    h1_sp = nc.dram_tensor("h1_sp", [NQ, 128, D], F32, kind=spill_kind).ap()
    x2_sp = nc.dram_tensor("x2_sp", [NQ, 128, D], F32, kind=spill_kind).ap()
    ln3_sp = nc.dram_tensor("ln3_sp", [NQ, 128, D], BF16, kind=spill_kind).ap()

    with ExitStack() as stack:
        En = stack.enter_context

        AR = 201 * KB
        arena = En(nc.sbuf_tensor("arena", [128, AR], U8))[:, :]

        def carve(off_kb, shape, dtype, partitions=128):
            nb = int(np.prod(shape[1:])) * mybir.dt.size(dtype)
            off = (int(off_kb * KB) + 63) // 64 * 64
            assert off + nb <= AR, (off_kb, shape, dtype)
            ap = arena[:, off:off + nb].bitcast(dtype)
            if len(shape) == 3:
                ap = ap.rearrange("p (a b) -> p a b", a=shape[1])
            elif len(shape) == 4:
                ap = ap.rearrange("p (a b c) -> p a b c", a=shape[1], b=shape[2])
            if partitions != 128:
                ap = ap[0:partitions]
            return ap

        eye = En(nc.sbuf_tensor("eye_s", [128, 128], BF16))[:, :]
        mask = En(nc.sbuf_tensor("mask_s", [128, 256], BF16))[:, :]
        bqkvc = En(nc.sbuf_tensor("bqkvc_s", [128, 24], F32))[:, :]
        bq8c = En(nc.sbuf_tensor("bq8c", [128, 8], F32))[:, :]
        bvr = En(nc.sbuf_tensor("bvr_s", [1, D], BF16))[:, :]
        bor = En(nc.sbuf_tensor("bor_s", [1, D], BF16))[:, :]
        ghost = En(nc.sbuf_tensor("ghost_s", [1, NH], F32))[:, :]
        ones_bf = En(nc.sbuf_tensor("ones_bf", [1, 128], BF16))[:, :]
        onesf = En(nc.sbuf_tensor("onesf", [128, 64], F32))[:, :]
        stats = En(nc.sbuf_tensor("stats", [128, 32], F32))[:, :]
        stats6 = En(nc.sbuf_tensor("stats6", [128, 48], F32))[:, :]
        tmpr = carve(154.56, [128, 512], F32)

        pb = [En(nc.psum_tensor(f"pb{i}", [128, 512], F32))[:, :] for i in range(7)]
        tpb = En(nc.psum_tensor("tpb", [128, 1024], BF16))[:, :]
        tpb3 = tpb.rearrange("p (a b) -> p a b", a=8)

        Q = {e: [] for e in ENGINES}

        def mksem(name):
            return Sem(En(nc.semaphore(name)))

        def op(eng, fn, sem=None, inc=1):
            if sem is None:
                Q[eng].append(fn)
                return None
            sem.n += inc
            v = sem.n
            Q[eng].append(lambda E, fn=fn, s=sem, i=inc: fn(E).then_inc(s.h, i))
            return v

        def wait(eng, sem, v):
            if v:
                Q[eng].append(lambda E, s=sem, v=v: E.wait_ge(s.h, v))

        bsems = {e: mksem(f"bar_{e}") for e in ENGINES}
        bar_scr = En(nc.sbuf_tensor("bar_scr", [1, 16], F32))[:, :]

        def barrier():
            vals = {}
            for e in ENGINES:
                if e == "dve":
                    vals[e] = op(e, lambda E: E.memset(bar_scr[0:1, 0:4], 0.0),
                                 sem=bsems[e])
                elif e == "act":
                    vals[e] = op(e, lambda E: E.activation(
                        out=bar_scr[0:1, 8:12], in_=bar_scr[0:1, 4:8],
                        func=AF.Identity), sem=bsems[e])
                else:
                    vals[e] = op(e, lambda E: E.nop(), sem=bsems[e])
            for e in ENGINES:
                for e2 in ENGINES:
                    if e2 != e:
                        wait(e, bsems[e2], vals[e2])

        s_small = mksem("s_small")
        sg_wg = mksem("sg_wg")
        sg_wu = mksem("sg_wu")
        sg_wd = mksem("sg_wd")
        sg_kvw = mksem("sg_kvw")
        sg_qw = mksem("sg_qw")
        sg_wo = mksem("sg_wo")
        s_x = [mksem(f"s_x{i}") for i in range(4)]
        s_h1w = [mksem(f"s_h1w{i}") for i in range(2)]
        s_ln2w = [mksem(f"s_ln2w{i}") for i in range(2)]
        s_ln2r = [mksem(f"s_ln2r{i}") for i in range(2)]
        s_lnqr = mksem("s_lnqr")
        s_h1r = [mksem(f"s_h1r{i}") for i in range(2)]
        s_x2w = [mksem(f"s_x2w{i}") for i in range(2)]
        s_ln3w = [mksem(f"s_ln3w{i}") for i in range(2)]
        s_ln3r = [mksem(f"s_ln3r{i}") for i in range(2)]
        s_x2r = [mksem(f"s_x2r{i}") for i in range(4)]
        s_out = [mksem(f"s_out{i}") for i in range(2)]
        pe = mksem("pe")
        act = mksem("act")
        dve = mksem("dve")

        op("sp", lambda E: E.dma_start(out=eye, in_=eye_h), sem=s_small, inc=16)
        op("sp", lambda E: E.dma_start(out=mask, in_=mask_h), sem=s_small, inc=16)
        op("sp", lambda E: E.dma_start(out=bqkvc, in_=bqkvc_h), sem=s_small, inc=16)
        op("sp", lambda E: E.dma_start(out=ghost, in_=ghost_h), sem=s_small, inc=16)
        op("po", lambda E: E.dma_start(out=bvr, in_=bvr_h), sem=s_small, inc=16)
        op("po", lambda E: E.dma_start(out=bor, in_=bor_h), sem=s_small, inc=16)
        v_small = s_small.n

        wgt = {"wg": carve(0, [128, 8, HID], BF16),
               "wu": carve(48, [128, 8, HID], BF16),
               "wd": carve(96, [128, 24, D], BF16)}
        op("po", lambda E, o=wgt["wg"]: E.dma_start(
            out=o, in_=wgT_h.rearrange("(c p) h -> p c h", p=128)), sem=sg_wg, inc=16)
        op("po", lambda E, o=wgt["wu"]: E.dma_start(
            out=o, in_=wuT_h.rearrange("(c p) h -> p c h", p=128)), sem=sg_wu, inc=16)
        op("po", lambda E, o=wgt["wd"]: E.dma_start(
            out=o, in_=wdT_h.rearrange("(c p) h -> p c h", p=128)), sem=sg_wd, inc=16)
        wg_load = [1]

        op("dve", lambda E: E.memset(ones_bf, 1.0))
        op("dve", lambda E: E.memset(onesf, 1.0))
        wait("dve", s_small, v_small)
        v_bq8 = op("dve", lambda E: E.tensor_scalar_mul(bq8c, bqkvc[:, 0:8], 0.125), sem=dve)

        def warmup(n=24):
            for _ in range(n):
                op("pe", lambda E: E.matmul(pb[0][:, 0:128], eye, eye,
                                            start=True, stop=True))

        ln_slot_free = [0, 0, 0, 0]
        ln_count = [0]

        def emit_ln(xa, out_ap, x_waits, out_war=()):
            r = ln_count[0] % 4
            ln_count[0] += 1
            st = stats[:, 8 * r:8 * r + 8]
            mean, var, srt, rstd, tmb = (st[:, i:i + 1] for i in range(5))
            st6 = stats6[:, 12 * r:12 * r + 12]
            for (s, v) in x_waits:
                wait("dve", s, v)
            wait("dve", act, ln_slot_free[r])
            op("dve", lambda E, o=st6[:, 0:6], xa=xa: E.bn_stats(o, xa[:, 0:512]))
            v_bs = op("dve", lambda E, o=st6[:, 6:12], xa=xa: E.bn_stats(
                o, xa[:, 512:1024]), sem=dve)
            wait("dve", dve, v_bs)
            v_ag = op("dve", lambda E, o=st[:, 0:2], i=st6: E.bn_aggr(o, i), sem=dve)
            wait("dve", dve, v_ag)
            v_ve = op("dve", lambda E, o=var: E.tensor_scalar_add(o, var, EPS), sem=dve)
            wait("act", dve, v_ve)
            v_sr = op("act", lambda E, i=var, o=srt: E.activation(
                o, i, AF.Sqrt), sem=act)
            wait("dve", act, v_sr)
            v_rr = op("dve", lambda E, i=srt, o=rstd: E.reciprocal(o, i), sem=dve)
            wait("dve", dve, v_rr)
            v_tmb = op("dve", lambda E, a=mean, b=rstd, o=tmb: E.scalar_tensor_tensor(
                o, a, -1.0, b, ALU.mult, ALU.mult), sem=dve)
            wait("act", dve, v_tmb)
            for (s_, v_) in out_war:
                wait("act", s_, v_)
            v_out = op("act", lambda E, xa=xa, o=out_ap, sc=rstd, bi=tmb: E.activation(
                o, xa, AF.Identity, bias=bi[:, 0:1], scale=sc[:, 0:1]), sem=act)
            ln_slot_free[r] = v_out
            return v_out

        tp_free = [0]

        def emit_transpose(ln_ap, lnT_dst, v_ready, rsem):
            wait("pe", rsem, v_ready) if rsem is not None else None
            wait("pe", dve, tp_free[0])
            v_tp = None
            for j in range(8):
                v_tp = op("pe", lambda E, o=tpb[:, 128 * j:128 * (j + 1)],
                          i=ln_ap[:, 128 * j:128 * (j + 1)]: E.transpose(o, i, eye),
                          sem=pe if j == 7 else None)
            wait("dve", pe, v_tp)
            v_ev = op("dve", lambda E, d=lnT_dst: E.tensor_copy(d, tpb3), sem=dve)
            tp_free[0] = v_ev
            return v_ev, v_tp

        def emit_ffn(lnT, v_lnT_ready, hp_off, sg_off, sink, war_free, wg=None):
            w_ = wgt
            hp = carve(hp_off, [128, 12, 256], BF16)
            sgb = [carve(sg_off + 0.5 * i, [128, 256], BF16) for i in range(2)]
            usb = [carve(sg_off + 1.0 + 0.5 * i, [128, 256], BF16) for i in range(2)]
            wait("pe", dve, v_lnT_ready)
            wait("pe", sg_wg, 16 * wg_load[0])
            v_hp = {}
            v_evac = {}
            last_down_v = 0
            for half in range(2):
                for k in range(12):
                    ht = 12 * half + k
                    if ht > 0:
                        wait("pe", act, v_evac[ht - 1][0])
                    v_g = None
                    for c in range(8):
                        v_g = op("pe", lambda E, o=w_["wg"][:, c, 128 * ht:128 * (ht + 1)],
                                 r=lnT[:, c, :], st=(c == 0), sp=(c == 7):
                                 E.matmul(pb[0][:, 0:256], o, r, start=st, stop=sp),
                                 sem=pe if c == 7 else None)
                    if ht == 0:
                        wait("pe", sg_wu, 16 * wg_load[0])
                    if ht > 0:
                        wait("pe", act, v_evac[ht - 1][1])
                    v_u = None
                    for c in range(8):
                        v_u = op("pe", lambda E, o=w_["wu"][:, c, 128 * ht:128 * (ht + 1)],
                                 r=lnT[:, c, :], st=(c == 0), sp=(c == 7):
                                 E.matmul(pb[1][:, 0:256], o, r, start=st, stop=sp),
                                 sem=pe if c == 7 else None)
                    wait("act", pe, v_g)
                    if ht >= 2:
                        wait("act", dve, v_hp[ht - 2])
                    v_sg = op("act", lambda E, o=sgb[ht % 2]: E.activation(
                        o, pb[0][:, 0:256], AF.Silu, scale=S127), sem=act)
                    wait("act", pe, v_u)
                    v_us = op("act", lambda E, o=usb[ht % 2]: E.activation(
                        o, pb[1][:, 0:256], AF.Identity, scale=0.5 * S127 * S127), sem=act)
                    v_evac[ht] = (v_sg, v_us)
                    wait("dve", act, v_us)
                    if half == 1:
                        wait("dve", pe, last_down_v)
                    v_hp[ht] = op("dve", lambda E, o=hp[:, ht % 12, :], a=sgb[ht % 2],
                                  b=usb[ht % 2]: E.tensor_tensor(o, a, b, ALU.mult), sem=dve)
                for k in range(12):
                    ht = 12 * half + k
                    wait("pe", dve, v_hp[ht])
                    if ht == 0:
                        wait("pe", sg_wd, 16 * wg_load[0])
                        for v in war_free:
                            wait("pe", dve, v)
                    st_ = (ht == 0)
                    sp_ = (ht == 23)
                    for tt in range(2):
                        lo, hi = [(pb[4], pb[5]), (pb[2], pb[3])][tt]
                        op("pe", lambda E, l=hp[:, ht % 12, 128 * tt:128 * (tt + 1)],
                           r=w_["wd"][:, ht, 0:512], o=lo, st=st_, sp=sp_:
                           E.matmul(o[:, 0:512], l, r, start=st, stop=sp))
                        last_down_v = op("pe", lambda E, l=hp[:, ht % 12, 128 * tt:128 * (tt + 1)],
                                         r=w_["wd"][:, ht, 512:1024], o=hi, st=st_, sp=sp_:
                                         E.matmul(o[:, 0:512], l, r, start=st, stop=sp),
                                         sem=pe)
            frees = []
            for tt in range(2):
                lo, hi = [(pb[4], pb[5]), (pb[2], pb[3])][tt]
                frees.append(sink(tt, lo, hi, last_down_v))
            return frees

        x_ring = [carve(144 + 4 * i, [128, D], F32) for i in range(4)]
        ln1_ring = [carve(193 + 2 * i, [128, D], BF16) for i in range(4)]
        lnT_ring = [carve(160 + 4 * i, [128, 8, 256], BF16) for i in range(2)]
        ln2_ring = [carve(176 + 2 * i, [128, D], BF16) for i in range(2)]
        h1_ring = [carve(180 + 4 * i, [128, D], F32) for i in range(2)]

        ms_h1add = {}
        ms_ln2 = {}
        ms_h1w = {}
        ms_ln2w = {}
        xld = {}
        v_ln1s = {}
        v_tp1 = {}
        war_free = []

        def p1_tile_head(t):
            if t >= 4:
                wait("sp", dve, ms_h1add[t - 4])
            xld[t] = op("sp", lambda E, o=x_ring[t % 4], i=x_h[t]: E.dma_start(out=o, in_=i),
                        sem=s_x[t % 4], inc=16)
            v_ln1s[t] = emit_ln(x_ring[t % 4], ln1_ring[t % 4], [(s_x[t % 4], xld[t])],
                                out_war=([(pe, v_tp1[t - 4])] if t >= 4 else ()))

        wait("pe", s_small, v_small)
        for t in (0, 1):
            p1_tile_head(t)
        for w in range(NT // 2):
            for t in (2 * w + 2, 2 * w + 3):
                if t < NT:
                    p1_tile_head(t)
            v_lnT_w = None
            for t in (2 * w, 2 * w + 1):
                v_lnT_w, v_tp1[t] = emit_transpose(
                    ln1_ring[t % 4],
                    lnT_ring[w % 2][:, :, 128 * (t % 2):128 * (t % 2 + 1)], v_ln1s[t], act)
            if w == 0:
                warmup()
            t0 = 2 * w

            def sink(tt, lo, hi, v_pe, t0=t0):
                tl = t0 + tt
                wait("dve", pe, v_pe)
                wait("dve", s_x[tl % 4], xld[tl])
                if tl >= 2:
                    wait("dve", act, ms_ln2[tl - 2])
                    if (tl - 2) % 2 == 1:
                        wait("dve", s_h1w[((tl - 2) // 2) % 2], ms_h1w[tl - 2])
                op("dve", lambda E, o=h1_ring[tl % 2][:, 0:512], a=lo[:, 0:512],
                   b=x_ring[tl % 4][:, 0:512]: E.tensor_tensor(o, a, b, ALU.add))
                v_h1 = op("dve", lambda E, o=h1_ring[tl % 2][:, 512:1024], a=hi[:, 0:512],
                          b=x_ring[tl % 4][:, 512:1024]: E.tensor_tensor(o, a, b, ALU.add),
                          sem=dve)
                ms_h1add[tl] = v_h1
                if tl % 2 == 1:
                    si = tl // 2
                    wait("sp", dve, v_h1)
                    if si >= 2:
                        wait("sp", s_h1w[si % 2], 16 * (si // 2))
                    ms_h1w[tl] = op("sp", lambda E, o=h1_sp[si], i=h1_ring[tl % 2]:
                                    E.dma_start(out=o, in_=i), sem=s_h1w[si % 2], inc=16)
                v_ln2 = emit_ln(h1_ring[tl % 2], ln2_ring[tl % 2], [(dve, v_h1)],
                                out_war=([(s_ln2w[tl % 2], ms_ln2w[tl - 2])]
                                         if tl >= 2 else ()))
                ms_ln2[tl] = v_ln2
                wait("sp", act, v_ln2)
                ms_ln2w[tl] = op("sp", lambda E, o=ln2_sp[tl], i=ln2_ring[tl % 2]:
                                 E.dma_start(out=o, in_=i), sem=s_ln2w[tl % 2], inc=16)
                return v_h1

            war_free = emit_ffn(lnT_ring[w % 2], v_lnT_w, 170.0, 168.0, sink, war_free)

        barrier()
        ln2w_tot = (s_ln2w[0].n, s_ln2w[1].n)
        h1w_tot = (s_h1w[0].n, s_h1w[1].n)

        K_sb = carve(0, [128, 8, 4096], BF16)
        V_sb = carve(64, [128, NT, NH, 65], BF16)
        kvw = carve(130.56, [128, 8, 2048], BF16)
        ln2w_ring = [carve(162.56 + 8 * i, [128, 4, D], BF16) for i in range(2)]
        ln2T_ring = [carve(178.56 + 8 * i, [128, 8, 512], BF16) for i in range(2)]

        op("po", lambda E: E.dma_start(
            out=kvw, in_=wqkvT_h[:, 1024:3072].rearrange("(c p) h -> p c h", p=128)),
            sem=sg_kvw, inc=16)
        v_vones = op("dve", lambda E: E.memset(V_sb[:, :, :, 64:65], 1.0), sem=dve)

        wait("sp", s_ln2w[0], ln2w_tot[0])
        wait("sp", s_ln2w[1], ln2w_tot[1])
        warmup()
        ms_kevac = {}
        ms_vevac = {}
        v_tr_last = {}
        for w in range(8):
            if w >= 2:
                wait("sp", pe, v_tr_last[w - 2])
            ld2 = op("sp", lambda E, o=ln2w_ring[w % 2],
                     i=ln2_sp[4 * w:4 * w + 4].rearrange("t p f -> p t f"):
                     E.dma_start(out=o, in_=i), sem=s_ln2r[w % 2], inc=16)
            v_lnT_w = None
            for tl in range(4):
                _, v_tp = emit_transpose(
                    ln2w_ring[w % 2][:, tl, :],
                    ln2T_ring[w % 2][:, :, 128 * tl:128 * (tl + 1)], ld2, s_ln2r[w % 2])
                v_tr_last[w] = v_tp
                v_lnT_w = tp_free[0]
            wait("pe", dve, v_lnT_w)
            wait("pe", sg_kvw, 16)
            wait("act", s_small, v_small)
            for j in range(8):
                bk = pb[j % 3]
                if j >= 3:
                    wait("pe", act, ms_kevac[(w, j - 3)])
                elif w > 0:
                    wait("pe", act, ms_kevac[(w - 1, j + 6 if j <= 1 else 5)])
                v_k = None
                for c in range(8):
                    v_k = op("pe", lambda E, o=bk[:, 0:512], l=kvw[:, c, 128 * j:128 * (j + 1)],
                             r=ln2T_ring[w % 2][:, c, :], st=(c == 0), sp=(c == 7):
                             E.matmul(o, l, r, start=st, stop=sp), sem=pe if c == 7 else None)
                wait("act", pe, v_k)
                ms_kevac[(w, j)] = op("act", lambda E, o=K_sb[:, j, 512 * w:512 * (w + 1)],
                                      i=bk[:, 0:512], b=bqkvc[:, 8 + j:9 + j]:
                                      E.activation(o, i, AF.Identity, bias=b[:, 0:1]),
                                      sem=act)
            for tl in range(4):
                t = 4 * w + tl
                if t >= 1:
                    wait("pe", act, ms_vevac[t - 1])
                wait("pe", s_small, v_small)
                v_v = None
                for c in range(8):
                    for hh in range(2):
                        op("pe", lambda E, o=pb[4 + hh][:, 0:512],
                           l=ln2T_ring[w % 2][:, c, 128 * tl:128 * (tl + 1)],
                           r=kvw[:, c, 1024 + 512 * hh:1024 + 512 * (hh + 1)], st=(c == 0):
                           E.matmul(o, l, r, start=st, stop=False))
                for hh in range(2):
                    v_v = op("pe", lambda E, o=pb[4 + hh][:, 0:512], l=ones_bf[0:1, 0:128],
                             r=bvr[0:1, 512 * hh:512 * (hh + 1)]:
                             E.matmul(o, l, r, start=False, stop=True),
                             sem=pe if hh == 1 else None)
                wait("act", pe, v_v)
                wait("act", dve, v_vones)
                op("act", lambda E, o=V_sb[:, t, 0:8, 0:64],
                   i=pb[4][:, 0:512].rearrange("p (a b) -> p a b", a=8):
                   E.activation(o, i, AF.Identity))
                ms_vevac[t] = op("act", lambda E, o=V_sb[:, t, 8:16, 0:64],
                                 i=pb[5][:, 0:512].rearrange("p (a b) -> p a b", a=8):
                                 E.activation(o, i, AF.Identity), sem=act)

        barrier()

        QA = carve(162.56, [128, NQ, 8, 128], BF16)
        qw = carve(130.56, [128, 8, 1024], BF16)
        lnq_ring = carve(146.56, [128, 4, D], BF16)
        lnqT = carve(154.56, [128, 8, 512], BF16)
        op("po", lambda E: E.dma_start(
            out=qw, in_=wqkvT_h[:, 0:1024].rearrange("(c p) h -> p c h", p=128)),
            sem=sg_qw, inc=16)
        ln2_odd = ln2_sp.rearrange("(a two) p f -> a two p f", two=2)
        warmup()
        ms_qevac = {}
        v_trq_last = 0
        for w in range(4):
            if w >= 1:
                wait("sp", pe, v_trq_last)
            ldq = op("sp", lambda E, o=lnq_ring,
                     i=ln2_odd[4 * w:4 * w + 4, 1:2].rearrange("t one p f -> p (t one) f"):
                     E.dma_start(out=o, in_=i), sem=s_lnqr, inc=16)
            v_lnT_w = None
            for tl in range(4):
                _, v_tp = emit_transpose(lnq_ring[:, tl, :],
                                         lnqT[:, :, 128 * tl:128 * (tl + 1)], ldq, s_lnqr)
                v_trq_last = v_tp
                v_lnT_w = tp_free[0]
            wait("pe", dve, v_lnT_w)
            wait("pe", sg_qw, 16)
            wait("act", dve, v_bq8)
            for j in range(8):
                bk = pb[j % 3]
                if j >= 3:
                    wait("pe", act, ms_qevac[(w, j - 3)])
                elif w > 0:
                    wait("pe", act, ms_qevac[(w - 1, j + 6 if j <= 1 else 5)])
                v_q = None
                for c in range(8):
                    v_q = op("pe", lambda E, o=bk[:, 0:512], l=qw[:, c, 128 * j:128 * (j + 1)],
                             r=lnqT[:, c, :], st=(c == 0), sp=(c == 7):
                             E.matmul(o, l, r, start=st, stop=sp), sem=pe if c == 7 else None)
                wait("act", pe, v_q)
                ms_qevac[(w, j)] = op("act", lambda E, o=QA[:, 4 * w:4 * w + 4, j, :],
                                      i=bk[:, 0:512].rearrange("p (a b) -> p a b", a=4),
                                      b=bq8c[:, j:j + 1]:
                                      E.activation(o, i, AF.Identity, bias=b[:, 0:1],
                                                   scale=0.125), sem=act)

        barrier()

        wo = carve(130.56, [128, 8, D], BF16)
        zbuf = [carve(146.56 + 1.0 * i, [128, 512], BF16) for i in range(4)]
        rbs_ring = [carve(150.56 + 2.0 * i, [128, 512], F32, partitions=64) for i in range(2)]
        op("po", lambda E: E.dma_start(
            out=wo, in_=woT_h.rearrange("(c p) d -> p c d", p=128)), sem=sg_wo, inc=16)

        st_lo = [pb[0], pb[1]]
        st_hi = [pb[6], pb[2]]
        a_banks = [pb[3], pb[4]]
        rbb = pb[5]
        st_rot = [0]
        z_rot = [0]
        ms_exp = {}
        ms_zcons = {}
        ms_afree = [0, 0]
        ms_rbfree = [0]

        wait("pe", s_small, v_small)
        wait("dve", s_small, v_small)
        warmup()

        def emit_qk(p, g, t, n256):
            j = g
            s0 = 2 * p
            si = st_rot[0]
            st_rot[0] += 1
            stl = st_lo[si % 2]
            sth = st_hi[si % 2]
            zi = z_rot[0]
            z_rot[0] += 1
            zb = zbuf[zi % 4]
            if si >= 2:
                wait("pe", act, ms_exp[si - 2])
            v_st = None
            for hh in range(2):
                base = 64 * hh
                if n256:
                    qs0, qs1 = s0, s0 + 2
                    ncol = 256
                else:
                    qs0, qs1 = s0 + 1, s0 + 2
                    ncol = 128
                v_st = op("pe", lambda E, o=(stl if hh == 0 else sth)[:, 0:ncol],
                          l=K_sb[base:base + 64, j, 128 * t:128 * (t + 1)],
                          r=QA[base:base + 64, qs0:qs1, j, :]:
                          E.matmul(o, l, r, start=True, stop=True),
                          sem=pe if hh == 1 else None)
            wait("act", pe, v_st)
            if zi >= 4:
                wait("act", pe, ms_zcons[zi - 4])
            half = 256 if n256 else 128
            op("act", lambda E, o=zb[:, 0:half], i=stl[:, 0:half]:
               E.activation(o, i, AF.Exp))
            v_z = op("act", lambda E, o=zb[:, half:2 * half], i=sth[:, 0:half]:
                     E.activation(o, i, AF.Exp), sem=act)
            ms_exp[si] = v_z
            gate_sem, gate_v = act, v_z
            masked = (t >= 4 * p)
            if masked:
                if n256:
                    mc = mask[:, 0:128] if t == 4 * p else mask[:, 128:256]
                    cols = [(0, 128), (256, 384)]
                else:
                    mc = mask[:, 0:128] if t == 4 * p + 2 else mask[:, 128:256]
                    cols = [(0, 128), (128, 256)]
                wait("dve", act, v_z)
                v_m = None
                for (c0, c1) in cols:
                    v_m = op("dve", lambda E, o=zb[:, c0:c1], m=mc:
                             E.tensor_tensor(o, o, m, ALU.mult),
                             sem=dve if (c0, c1) == cols[-1] else None)
                gate_sem, gate_v = dve, v_m
            return (zi, zb, gate_sem, gate_v, n256)

        def emit_sv(p, g, t, gate, first, last):
            zi, zb, gate_sem, gate_v, n256 = gate
            abank = a_banks[g % 2]
            wait("pe", gate_sem, gate_v)
            if first:
                wait("pe", dve, ms_afree[g % 2])
            v_sv = None
            for hh in range(2):
                h = 2 * g + hh
                if n256:
                    ac, zc, ncol = 256 * hh, 256 * hh, 256
                else:
                    ac, zc, ncol = 256 * hh + 128, 128 * hh, 128
                v_sv = op("pe", lambda E, o=abank[0:65, ac:ac + ncol],
                          l=V_sb[:, t, h, 0:65], r=zb[:, zc:zc + ncol],
                          st=(first and hh == 0), sp=last:
                          E.matmul(o, l, r, start=st, stop=sp),
                          sem=pe if hh == 1 else None)
            ms_zcons[zi] = v_sv
            return v_sv

        tmpb = carve(156.56, [1, 512], BF16, partitions=1)

        def emit_evac(p, g, v_sv):
            abank = a_banks[g % 2]
            ri = g % 2
            wait("dve", pe, v_sv)
            v_gh = None
            for hh in range(2):
                h = 2 * g + hh
                v_gh = op("dve", lambda E, o=tmpr[0:1, 256 * hh:256 * hh + 256],
                          i=abank[64:65, 256 * hh:256 * hh + 256], gh=ghost[0:1, h:h + 1]:
                          E.tensor_scalar(o, i, gh, None, ALU.add),
                          sem=dve if hh == 1 else None)
            wait("dve", dve, v_gh)

            def _recip_bf16(E):
                with nc.allow_low_precision(reason="normalizer feeds bf16 operand"):
                    return E.reciprocal(tmpb[0:1, 0:512], tmpr[0:1, 0:512])

            v_rr = op("dve", _recip_bf16, sem=dve)
            wait("pe", dve, v_rr)
            wait("pe", dve, ms_rbfree[0])
            v_rb = op("pe", lambda E, o=rbb[0:64, 0:512]: E.matmul(
                o, ones_bf[0:1, 0:64], tmpb[0:1, 0:512], start=True, stop=True), sem=pe)
            wait("dve", pe, v_rb)
            v_c = op("dve", lambda E, o=rbs_ring[ri], i=rbb[0:64, 0:512]:
                     E.tensor_copy(o, i), sem=dve)
            s0 = 2 * p
            v_a = None
            for hh in range(2):
                dst = QA[64 * hh:64 * hh + 64, s0:s0 + 2, g, :]
                v_a = op("dve", lambda E, o=dst,
                         a=abank[0:64, 256 * hh:256 * hh + 256].rearrange(
                             "p (a b) -> p a b", a=2),
                         b=rbs_ring[ri][0:64, 256 * hh:256 * hh + 256].rearrange(
                             "p (a b) -> p a b", a=2):
                         E.tensor_tensor(o, a, b, ALU.mult), sem=dve)
            ms_afree[g % 2] = v_a
            ms_rbfree[0] = v_c

        pairs = [(p, a) for p in range(8) for a in range(4)]
        pending = None
        for p, a in pairs:
            T = 4 * p + 4
            visits = [(t, t < 4 * p + 2) for t in range(T)]
            g0, g1 = 2 * a, 2 * a + 1
            gates0 = {0: emit_qk(p, g0, *visits[0])}
            gates1 = {0: emit_qk(p, g1, *visits[0])}
            if pending is not None:
                emit_evac(*pending[0])
                emit_evac(*pending[1])
                pending = None
            v_sv0 = v_sv1 = None
            for i in range(T):
                if i + 1 < T:
                    gates0[i + 1] = emit_qk(p, g0, *visits[i + 1])
                    gates1[i + 1] = emit_qk(p, g1, *visits[i + 1])
                t, _ = visits[i]
                v_sv0 = emit_sv(p, g0, t, gates0.pop(i), i == 0, i == T - 1)
                v_sv1 = emit_sv(p, g1, t, gates1.pop(i), i == 0, i == T - 1)
            pending = ((p, g0, v_sv0), (p, g1, v_sv1))
        emit_evac(*pending[0])
        emit_evac(*pending[1])

        barrier()

        h1in_ring = [carve(0 + 4 * i, [128, D], F32) for i in range(2)]
        x2_ring = [carve(8 + 4 * i, [128, D], F32) for i in range(2)]
        ln3_ring = [carve(146.56 + 2 * i, [128, D], BF16) for i in range(2)]
        wgt2 = {"wg": carve(16, [128, 8, HID], BF16),
                "wu": carve(64, [128, 8, HID], BF16)}
        op("po", lambda E, o=wgt2["wg"]: E.dma_start(
            out=o, in_=wgT_h.rearrange("(c p) h -> p c h", p=128)), sem=sg_wg, inc=16)
        op("po", lambda E, o=wgt2["wu"]: E.dma_start(
            out=o, in_=wuT_h.rearrange("(c p) h -> p c h", p=128)), sem=sg_wu, inc=16)

        ms_x2 = {}
        ms_ln3 = {}
        ms_ln3w = {}
        wait("pe", sg_wo, 16)
        warmup()
        for s in range(NQ):
            if s >= 2:
                wait("sp", dve, ms_x2[s - 2])
                wait("sp", s_h1r[s % 2], 16 * (s // 2))
            if s == 0:
                wait("sp", s_h1w[0], h1w_tot[0])
                wait("sp", s_h1w[1], h1w_tot[1])
            ldh = op("sp", lambda E, o=h1in_ring[s % 2], i=h1_sp[s]: E.dma_start(out=o, in_=i),
                     sem=s_h1r[s % 2], inc=16)
            if s >= 1:
                wait("pe", dve, ms_x2[s - 1])
            v_wo = None
            for half in range(2):
                bank = pb[4 + half]
                for c in range(8):
                    op("pe", lambda E, o=bank[:, 0:512], l=QA[:, s, c, :],
                       r=wo[:, c, 512 * half:512 * (half + 1)], st=(c == 0):
                       E.matmul(o, l, r, start=st, stop=False))
                v_wo = op("pe", lambda E, o=bank[:, 0:512], l=ones_bf[0:1, 0:128],
                          r=bor[0:1, 512 * half:512 * (half + 1)]:
                          E.matmul(o, l, r, start=False, stop=True), sem=pe)
            wait("dve", pe, v_wo)
            wait("dve", s_h1r[s % 2], ldh)
            if s >= 2:
                wait("dve", act, ms_ln3[s - 2])
                wait("dve", s_x2w[s % 2], 16 * (s // 2))
            op("dve", lambda E, o=x2_ring[s % 2][:, 0:512], a=pb[4][:, 0:512],
               b=h1in_ring[s % 2][:, 0:512]: E.tensor_tensor(o, a, b, ALU.add))
            v_x2 = op("dve", lambda E, o=x2_ring[s % 2][:, 512:1024], a=pb[5][:, 0:512],
                      b=h1in_ring[s % 2][:, 512:1024]: E.tensor_tensor(o, a, b, ALU.add),
                      sem=dve)
            ms_x2[s] = v_x2
            wait("sp", dve, v_x2)
            op("sp", lambda E, o=x2_sp[s], i=x2_ring[s % 2]: E.dma_start(out=o, in_=i),
               sem=s_x2w[s % 2], inc=16)
            v_ln3 = emit_ln(x2_ring[s % 2], ln3_ring[s % 2], [(dve, v_x2)],
                            out_war=([(s_ln3w[s % 2], ms_ln3w[s - 2])] if s >= 2 else ()))
            ms_ln3[s] = v_ln3
            wait("sp", act, v_ln3)
            ms_ln3w[s] = op("sp", lambda E, o=ln3_sp[s], i=ln3_ring[s % 2]:
                            E.dma_start(out=o, in_=i), sem=s_ln3w[s % 2], inc=16)

        barrier()
        x2w_tot = (s_x2w[0].n, s_x2w[1].n)
        ln3w_tot = (s_ln3w[0].n, s_ln3w[1].n)

        wgt2["wd"] = carve(112, [128, 24, D], BF16)
        op("po", lambda E, o=wgt2["wd"]: E.dma_start(
            out=o, in_=wdT_h.rearrange("(c p) h -> p c h", p=128)), sem=sg_wd, inc=16)
        wgt = wgt2
        wg_load[0] = 2
        x2in_ring = [carve(0 + 4 * i, [128, D], F32) for i in range(4)]
        out_ring = [carve(160 + 4 * i, [128, D], F32) for i in range(2)]
        ln3in_ring = [carve(168 + 4 * i, [128, 2, D], BF16) for i in range(2)]
        lnT5_ring = [carve(176 + 4 * i, [128, 8, 256], BF16) for i in range(2)]

        war_free = []
        ms_out = {}
        v_tr5 = {}
        xld5 = {}
        warmup()
        for w in range(8):
            t0 = 2 * w
            if w == 0:
                wait("sp", s_ln3w[0], ln3w_tot[0])
                wait("sp", s_ln3w[1], ln3w_tot[1])
                wait("sp", s_x2w[0], x2w_tot[0])
                wait("sp", s_x2w[1], x2w_tot[1])
            if w >= 2:
                wait("sp", pe, v_tr5[w - 2])
                wait("sp", s_ln3r[w % 2], 16 * (w // 2))
            ldl = op("sp", lambda E, o=ln3in_ring[w % 2],
                     i=ln3_sp[t0:t0 + 2].rearrange("t p f -> p t f"):
                     E.dma_start(out=o, in_=i), sem=s_ln3r[w % 2], inc=16)
            for tt in range(2):
                s = t0 + tt
                if s >= 4:
                    wait("sp", dve, ms_out[s - 4])
                    wait("sp", s_x2r[s % 4], 16 * (s // 4))
                xld5[s] = op("sp", lambda E, o=x2in_ring[s % 4], i=x2_sp[s]:
                             E.dma_start(out=o, in_=i), sem=s_x2r[s % 4], inc=16)
            v_lnT_w = None
            for tt in range(2):
                _, v_tp = emit_transpose(ln3in_ring[w % 2][:, tt, :],
                                         lnT5_ring[w % 2][:, :, 128 * tt:128 * (tt + 1)],
                                         ldl, s_ln3r[w % 2])
                v_tr5[w] = v_tp
                v_lnT_w = tp_free[0]

            def sink5(tt, lo, hi, v_pe, t0=t0):
                s = t0 + tt
                wait("dve", pe, v_pe)
                wait("dve", s_x2r[s % 4], xld5[s])
                if s >= 2:
                    wait("dve", s_out[s % 2], 16 * (s // 2))
                op("dve", lambda E, o=out_ring[s % 2][:, 0:512], a=lo[:, 0:512],
                   b=x2in_ring[s % 4][:, 0:512]: E.tensor_tensor(o, a, b, ALU.add))
                v_o = op("dve", lambda E, o=out_ring[s % 2][:, 512:1024], a=hi[:, 0:512],
                         b=x2in_ring[s % 4][:, 512:1024]: E.tensor_tensor(o, a, b, ALU.add),
                         sem=dve)
                ms_out[s] = v_o
                wait("sp", dve, v_o)
                op("sp", lambda E, o=out_h[s], i=out_ring[s % 2]:
                   E.dma_start(out=o, in_=i), sem=s_out[s % 2], inc=16)
                return v_o

            war_free = emit_ffn(lnT5_ring[w % 2], v_lnT_w, 186.0, 184.0, sink5, war_free)

        wait("sp", s_out[0], s_out[0].n)
        wait("sp", s_out[1], s_out[1].n)

        with nc.Block() as block:
            block.sync(lambda e, ops=Q["sp"]: [f(e) for f in ops])
            block.tensor(lambda e, ops=Q["pe"]: [f(e) for f in ops])
            block.scalar(lambda e, ops=Q["act"]: [f(e) for f in ops])
            block.vector(lambda e, ops=Q["dve"]: [f(e) for f in ops])
            block.gpsimd(lambda e, ops=Q["po"]: [f(e) for f in ops])

    return nc


def _stage_inputs(inputs):
    x = np.asarray(inputs["L0_attn_inputs"], dtype=np.float32)
    wqkv = np.asarray(inputs["L0_attn_Wqkv_w"], dtype=np.float32)
    bqkv = np.asarray(inputs["L0_attn_Wqkv_b"], dtype=np.float32)
    wo_w = np.asarray(inputs["L0_attn_Wo_w"], dtype=np.float32)
    bo = np.asarray(inputs["L0_attn_Wo_b"], dtype=np.float32)
    ghost = np.asarray(inputs["L0_attn_ghost"], dtype=np.float32)
    wg = np.asarray(inputs["L0_ffn_w_gate"], dtype=np.int32)
    wu = np.asarray(inputs["L0_ffn_w_up"], dtype=np.int32)
    wdn = np.asarray(inputs["L0_ffn_w_down"], dtype=np.int32)

    shared = {
        "wgT": np.ascontiguousarray(wg.T),
        "wuT": np.ascontiguousarray(wu.T),
        "wdT": np.ascontiguousarray(wdn.T),
        "wqkvT": np.ascontiguousarray(wqkv.T),
        "woT": np.ascontiguousarray(wo_w.T),
        "bqkvc": np.ascontiguousarray(bqkv.reshape(24, 128).T),
        "bvr": np.ascontiguousarray(bqkv[2048:].reshape(1, D)),
        "bor": np.ascontiguousarray(bo.reshape(1, D)),
        "ghost": np.ascontiguousarray(ghost.reshape(1, NH)),
        "eye": np.eye(128, dtype=ml_dtypes.bfloat16),
    }
    tri = np.triu(np.ones((128, 128), dtype=np.float32)).astype(ml_dtypes.bfloat16)
    ones = np.ones((128, 128), dtype=ml_dtypes.bfloat16)
    zeros = np.zeros((128, 128), dtype=ml_dtypes.bfloat16)
    mask_u = np.ascontiguousarray(np.concatenate([ones, tri], axis=1))
    mask_v = np.ascontiguousarray(np.concatenate([zeros, tri], axis=1))

    perm_v = np.arange(NT)
    perm_v[0::2] += 1
    perm_v[1::2] -= 1

    in_maps = []
    for c in range(8):
        b = c // 2
        xt = x[b].reshape(NT, 128, D)
        if c % 2 == 1:
            xt = xt[perm_v]
        m = dict(shared)
        m["x"] = np.ascontiguousarray(xt)
        m["mask"] = mask_u if c % 2 == 0 else mask_v
        in_maps.append(m)
    return in_maps


def _assemble(results):
    full = np.empty((4, NT * 128, D), dtype=np.float32)
    for c in range(8):
        b = c // 2
        o = np.asarray(results[c]["out"], dtype=np.float32)
        for s in range(NQ):
            j = 2 * s + 1 if c % 2 == 0 else 2 * s
            full[b, 128 * j:128 * (j + 1), :] = o[s]
    return full


def kernel(**inputs):
    nc = build_program()
    in_maps = _stage_inputs(inputs)
    res = run_bass_kernel_spmd(nc, in_maps, core_ids=list(range(8)))
    return _assemble(res.results)


# version 09
# speedup vs baseline: 1.0694x; 1.0240x over previous
import numpy as np
import ml_dtypes
from contextlib import ExitStack

import concourse.bass as bass
import concourse.mybir as mybir
from concourse.bass_utils import run_bass_kernel_spmd

F32 = mybir.dt.float32
BF16 = mybir.dt.bfloat16
I32 = mybir.dt.int32
U8 = mybir.dt.uint8
AF = mybir.ActivationFunctionType
ALU = mybir.AluOpType
AX = mybir.AxisListType.X

D = 1024
HID = 3072
NH = 16
NT = 32
NQ = 16
S127 = float(np.float32(1.0 / 127.0))
EPS = 1e-5
KB = 1024

DEBUG = 0

ENGINES = ("sp", "pe", "act", "dve", "po")


class Sem:
    __slots__ = ("h", "n")

    def __init__(self, h):
        self.h = h
        self.n = 0


def build_program():
    nc = bass.Bass("TRN2", target_bir_lowering=False, debug=False)

    x_h = nc.dram_tensor("x", [NT, 128, D], F32, kind="ExternalInput").ap()
    wgT_h = nc.dram_tensor("wgT", [D, HID], I32, kind="ExternalInput").ap()
    wuT_h = nc.dram_tensor("wuT", [D, HID], I32, kind="ExternalInput").ap()
    wdT_h = nc.dram_tensor("wdT", [HID, D], I32, kind="ExternalInput").ap()
    wqkvT_h = nc.dram_tensor("wqkvT", [D, HID], F32, kind="ExternalInput").ap()
    woT_h = nc.dram_tensor("woT", [D, D], F32, kind="ExternalInput").ap()
    bqkvc_h = nc.dram_tensor("bqkvc", [128, 24], F32, kind="ExternalInput").ap()
    bvr_h = nc.dram_tensor("bvr", [1, D], F32, kind="ExternalInput").ap()
    bor_h = nc.dram_tensor("bor", [1, D], F32, kind="ExternalInput").ap()
    ghost_h = nc.dram_tensor("ghost", [1, NH], F32, kind="ExternalInput").ap()
    mask_h = nc.dram_tensor("mask", [128, 256], BF16, kind="ExternalInput").ap()
    eye_h = nc.dram_tensor("eye", [128, 128], BF16, kind="ExternalInput").ap()

    out_h = nc.dram_tensor("out", [NQ, 128, D], F32, kind="ExternalOutput").ap()
    wspg_h = nc.dram_tensor("wspg", [128, 8, 3072], BF16, kind="Internal").ap()
    wspu_h = nc.dram_tensor("wspu", [128, 8, 3072], BF16, kind="Internal").ap()
    wspd_h = nc.dram_tensor("wspd", [128, 24, 1024], BF16, kind="Internal").ap()

    spill_kind = "ExternalOutput" if DEBUG else "Internal"
    ln2_sp = nc.dram_tensor("ln2_sp", [NT, 128, D], BF16, kind=spill_kind).ap()
    h1_sp = nc.dram_tensor("h1_sp", [NQ, 128, D], F32, kind=spill_kind).ap()
    x2_sp = nc.dram_tensor("x2_sp", [NQ, 128, D], F32, kind=spill_kind).ap()
    ln3_sp = nc.dram_tensor("ln3_sp", [NQ, 128, D], BF16, kind=spill_kind).ap()

    with ExitStack() as stack:
        En = stack.enter_context

        AR = 201 * KB
        arena = En(nc.sbuf_tensor("arena", [128, AR], U8))[:, :]

        def carve(off_kb, shape, dtype, partitions=128):
            nb = int(np.prod(shape[1:])) * mybir.dt.size(dtype)
            off = (int(off_kb * KB) + 63) // 64 * 64
            assert off + nb <= AR, (off_kb, shape, dtype)
            ap = arena[:, off:off + nb].bitcast(dtype)
            if len(shape) == 3:
                ap = ap.rearrange("p (a b) -> p a b", a=shape[1])
            elif len(shape) == 4:
                ap = ap.rearrange("p (a b c) -> p a b c", a=shape[1], b=shape[2])
            if partitions != 128:
                ap = ap[0:partitions]
            return ap

        eye = En(nc.sbuf_tensor("eye_s", [128, 128], BF16))[:, :]
        mask = En(nc.sbuf_tensor("mask_s", [128, 256], BF16))[:, :]
        bqkvc = En(nc.sbuf_tensor("bqkvc_s", [128, 24], F32))[:, :]
        bq8c = En(nc.sbuf_tensor("bq8c", [128, 8], F32))[:, :]
        bvr = En(nc.sbuf_tensor("bvr_s", [1, D], BF16))[:, :]
        bor = En(nc.sbuf_tensor("bor_s", [1, D], BF16))[:, :]
        ghost = En(nc.sbuf_tensor("ghost_s", [1, NH], F32))[:, :]
        ones_bf = En(nc.sbuf_tensor("ones_bf", [1, 128], BF16))[:, :]
        onesf = En(nc.sbuf_tensor("onesf", [128, 64], F32))[:, :]
        stats = En(nc.sbuf_tensor("stats", [128, 32], F32))[:, :]
        stats6 = En(nc.sbuf_tensor("stats6", [128, 48], F32))[:, :]
        tmpr = carve(154.56, [128, 512], F32)

        pb = [En(nc.psum_tensor(f"pb{i}", [128, 512], F32))[:, :] for i in range(7)]
        tpb = En(nc.psum_tensor("tpb", [128, 1024], BF16))[:, :]
        tpb3 = tpb.rearrange("p (a b) -> p a b", a=8)

        Q = {e: [] for e in ENGINES}

        def mksem(name):
            return Sem(En(nc.semaphore(name)))

        def op(eng, fn, sem=None, inc=1):
            if sem is None:
                Q[eng].append(fn)
                return None
            sem.n += inc
            v = sem.n
            Q[eng].append(lambda E, fn=fn, s=sem, i=inc: fn(E).then_inc(s.h, i))
            return v

        def wait(eng, sem, v):
            if v:
                Q[eng].append(lambda E, s=sem, v=v: E.wait_ge(s.h, v))

        bsems = {e: mksem(f"bar_{e}") for e in ENGINES}
        bar_scr = En(nc.sbuf_tensor("bar_scr", [1, 16], F32))[:, :]

        def barrier():
            vals = {}
            for e in ENGINES:
                if e == "dve":
                    vals[e] = op(e, lambda E: E.memset(bar_scr[0:1, 0:4], 0.0),
                                 sem=bsems[e])
                elif e == "act":
                    vals[e] = op(e, lambda E: E.activation(
                        out=bar_scr[0:1, 8:12], in_=bar_scr[0:1, 4:8],
                        func=AF.Identity), sem=bsems[e])
                else:
                    vals[e] = op(e, lambda E: E.nop(), sem=bsems[e])
            for e in ENGINES:
                for e2 in ENGINES:
                    if e2 != e:
                        wait(e, bsems[e2], vals[e2])

        s_small = mksem("s_small")
        sg_wg = mksem("sg_wg")
        sg_wu = mksem("sg_wu")
        sg_wd = mksem("sg_wd")
        sg_kvw = mksem("sg_kvw")
        sg_qw = mksem("sg_qw")
        sg_wo = mksem("sg_wo")
        s_x = [mksem(f"s_x{i}") for i in range(4)]
        s_h1w = [mksem(f"s_h1w{i}") for i in range(2)]
        s_ln2w = [mksem(f"s_ln2w{i}") for i in range(2)]
        s_ln2r = [mksem(f"s_ln2r{i}") for i in range(2)]
        s_lnqr = mksem("s_lnqr")
        s_h1r = [mksem(f"s_h1r{i}") for i in range(2)]
        s_x2w = [mksem(f"s_x2w{i}") for i in range(2)]
        s_ln3w = [mksem(f"s_ln3w{i}") for i in range(2)]
        s_ln3r = [mksem(f"s_ln3r{i}") for i in range(2)]
        s_x2r = [mksem(f"s_x2r{i}") for i in range(4)]
        s_out = [mksem(f"s_out{i}") for i in range(2)]
        pe = mksem("pe")
        act = mksem("act")
        dve = mksem("dve")

        op("sp", lambda E: E.dma_start(out=eye, in_=eye_h), sem=s_small, inc=16)
        op("sp", lambda E: E.dma_start(out=mask, in_=mask_h), sem=s_small, inc=16)
        op("sp", lambda E: E.dma_start(out=bqkvc, in_=bqkvc_h), sem=s_small, inc=16)
        op("sp", lambda E: E.dma_start(out=ghost, in_=ghost_h), sem=s_small, inc=16)
        op("po", lambda E: E.dma_start(out=bvr, in_=bvr_h), sem=s_small, inc=16)
        op("po", lambda E: E.dma_start(out=bor, in_=bor_h), sem=s_small, inc=16)
        v_small = s_small.n

        wgt = {"wg": carve(0, [128, 8, HID], BF16),
               "wu": carve(48, [128, 8, HID], BF16),
               "wd": carve(96, [128, 24, D], BF16)}
        op("po", lambda E, o=wgt["wg"]: E.dma_start(
            out=o, in_=wgT_h.rearrange("(c p) h -> p c h", p=128)), sem=sg_wg, inc=16)
        op("po", lambda E, o=wgt["wu"]: E.dma_start(
            out=o, in_=wuT_h.rearrange("(c p) h -> p c h", p=128)), sem=sg_wu, inc=16)
        op("po", lambda E, o=wgt["wd"]: E.dma_start(
            out=o, in_=wdT_h.rearrange("(c p) h -> p c h", p=128)), sem=sg_wd, inc=16)
        wg_load = [1]

        s_wsp = mksem("s_wsp")
        wait("po", sg_wg, 16)
        op("po", lambda E, i=wgt["wg"]: E.dma_start(out=wspg_h, in_=i), sem=s_wsp, inc=16)
        wait("po", sg_wu, 16)
        op("po", lambda E, i=wgt["wu"]: E.dma_start(out=wspu_h, in_=i), sem=s_wsp, inc=16)
        wait("po", sg_wd, 16)
        op("po", lambda E, i=wgt["wd"]: E.dma_start(out=wspd_h, in_=i), sem=s_wsp, inc=16)

        op("dve", lambda E: E.memset(ones_bf, 1.0))
        op("dve", lambda E: E.memset(onesf, 1.0))
        wait("dve", s_small, v_small)
        v_bq8 = op("dve", lambda E: E.tensor_scalar_mul(bq8c, bqkvc[:, 0:8], 0.125), sem=dve)

        def warmup(n=24):
            for _ in range(n):
                op("pe", lambda E: E.matmul(pb[0][:, 0:128], eye, eye,
                                            start=True, stop=True))

        ln_slot_free = [0, 0, 0, 0]
        ln_count = [0]

        def emit_ln(xa, out_ap, x_waits, out_war=()):
            r = ln_count[0] % 4
            ln_count[0] += 1
            st = stats[:, 8 * r:8 * r + 8]
            mean, var, srt, rstd, tmb = (st[:, i:i + 1] for i in range(5))
            st6 = stats6[:, 12 * r:12 * r + 12]
            for (s, v) in x_waits:
                wait("dve", s, v)
            wait("dve", act, ln_slot_free[r])
            op("dve", lambda E, o=st6[:, 0:6], xa=xa: E.bn_stats(o, xa[:, 0:512]))
            v_bs = op("dve", lambda E, o=st6[:, 6:12], xa=xa: E.bn_stats(
                o, xa[:, 512:1024]), sem=dve)
            wait("dve", dve, v_bs)
            v_ag = op("dve", lambda E, o=st[:, 0:2], i=st6: E.bn_aggr(o, i), sem=dve)
            wait("dve", dve, v_ag)
            v_ve = op("dve", lambda E, o=var: E.tensor_scalar_add(o, var, EPS), sem=dve)
            wait("act", dve, v_ve)
            v_sr = op("act", lambda E, i=var, o=srt: E.activation(
                o, i, AF.Sqrt), sem=act)
            wait("dve", act, v_sr)
            v_rr = op("dve", lambda E, i=srt, o=rstd: E.reciprocal(o, i), sem=dve)
            wait("dve", dve, v_rr)
            v_tmb = op("dve", lambda E, a=mean, b=rstd, o=tmb: E.scalar_tensor_tensor(
                o, a, -1.0, b, ALU.mult, ALU.mult), sem=dve)
            wait("act", dve, v_tmb)
            for (s_, v_) in out_war:
                wait("act", s_, v_)
            v_out = op("act", lambda E, xa=xa, o=out_ap, sc=rstd, bi=tmb: E.activation(
                o, xa, AF.Identity, bias=bi[:, 0:1], scale=sc[:, 0:1]), sem=act)
            ln_slot_free[r] = v_out
            return v_out

        tp_free = [0]

        def emit_transpose(ln_ap, lnT_dst, v_ready, rsem):
            wait("pe", rsem, v_ready) if rsem is not None else None
            wait("pe", dve, tp_free[0])
            v_tp = None
            for j in range(8):
                v_tp = op("pe", lambda E, o=tpb[:, 128 * j:128 * (j + 1)],
                          i=ln_ap[:, 128 * j:128 * (j + 1)]: E.transpose(o, i, eye),
                          sem=pe if j == 7 else None)
            wait("dve", pe, v_tp)
            v_ev = op("dve", lambda E, d=lnT_dst: E.tensor_copy(d, tpb3), sem=dve)
            tp_free[0] = v_ev
            return v_ev, v_tp

        def emit_ffn(lnT, v_lnT_ready, hp_off, sg_off, sink, war_free, wg=None):
            w_ = wgt
            hp = carve(hp_off, [128, 12, 256], BF16)
            sgb = [carve(sg_off + 0.5 * i, [128, 256], BF16) for i in range(2)]
            usb = [carve(sg_off + 1.0 + 0.5 * i, [128, 256], BF16) for i in range(2)]
            wait("pe", dve, v_lnT_ready)
            wait("pe", sg_wg, 16 * wg_load[0])
            v_hp = {}
            v_evac = {}
            last_down_v = 0
            for half in range(2):
                for k in range(12):
                    ht = 12 * half + k
                    if ht > 0:
                        wait("pe", act, v_evac[ht - 1][0])
                    v_g = None
                    for c in range(8):
                        v_g = op("pe", lambda E, o=w_["wg"][:, c, 128 * ht:128 * (ht + 1)],
                                 r=lnT[:, c, :], st=(c == 0), sp=(c == 7):
                                 E.matmul(pb[0][:, 0:256], o, r, start=st, stop=sp),
                                 sem=pe if c == 7 else None)
                    if ht == 0:
                        wait("pe", sg_wu, 16 * wg_load[0])
                    if ht > 0:
                        wait("pe", act, v_evac[ht - 1][1])
                    v_u = None
                    for c in range(8):
                        v_u = op("pe", lambda E, o=w_["wu"][:, c, 128 * ht:128 * (ht + 1)],
                                 r=lnT[:, c, :], st=(c == 0), sp=(c == 7):
                                 E.matmul(pb[1][:, 0:256], o, r, start=st, stop=sp),
                                 sem=pe if c == 7 else None)
                    wait("act", pe, v_g)
                    if ht >= 2:
                        wait("act", dve, v_hp[ht - 2])
                    v_sg = op("act", lambda E, o=sgb[ht % 2]: E.activation(
                        o, pb[0][:, 0:256], AF.Silu, scale=S127), sem=act)
                    wait("act", pe, v_u)
                    v_us = op("act", lambda E, o=usb[ht % 2]: E.activation(
                        o, pb[1][:, 0:256], AF.Identity, scale=0.5 * S127 * S127), sem=act)
                    v_evac[ht] = (v_sg, v_us)
                    wait("dve", act, v_us)
                    if half == 1:
                        wait("dve", pe, last_down_v)
                    v_hp[ht] = op("dve", lambda E, o=hp[:, ht % 12, :], a=sgb[ht % 2],
                                  b=usb[ht % 2]: E.tensor_tensor(o, a, b, ALU.mult), sem=dve)
                for k in range(12):
                    ht = 12 * half + k
                    wait("pe", dve, v_hp[ht])
                    if ht == 0:
                        wait("pe", sg_wd, 16 * wg_load[0])
                        for v in war_free:
                            wait("pe", dve, v)
                    st_ = (ht == 0)
                    sp_ = (ht == 23)
                    for tt in range(2):
                        lo, hi = [(pb[4], pb[5]), (pb[2], pb[3])][tt]
                        op("pe", lambda E, l=hp[:, ht % 12, 128 * tt:128 * (tt + 1)],
                           r=w_["wd"][:, ht, 0:512], o=lo, st=st_, sp=sp_:
                           E.matmul(o[:, 0:512], l, r, start=st, stop=sp))
                        last_down_v = op("pe", lambda E, l=hp[:, ht % 12, 128 * tt:128 * (tt + 1)],
                                         r=w_["wd"][:, ht, 512:1024], o=hi, st=st_, sp=sp_:
                                         E.matmul(o[:, 0:512], l, r, start=st, stop=sp),
                                         sem=pe)
            frees = []
            for tt in range(2):
                lo, hi = [(pb[4], pb[5]), (pb[2], pb[3])][tt]
                frees.append(sink(tt, lo, hi, last_down_v))
            return frees

        x_ring = [carve(144 + 4 * i, [128, D], F32) for i in range(4)]
        ln1_ring = [carve(193 + 2 * i, [128, D], BF16) for i in range(4)]
        lnT_ring = [carve(160 + 4 * i, [128, 8, 256], BF16) for i in range(2)]
        ln2_ring = [carve(176 + 2 * i, [128, D], BF16) for i in range(2)]
        h1_ring = [carve(180 + 4 * i, [128, D], F32) for i in range(2)]

        ms_h1add = {}
        ms_ln2 = {}
        ms_h1w = {}
        ms_ln2w = {}
        xld = {}
        v_ln1s = {}
        v_tp1 = {}
        war_free = []

        def p1_tile_head(t):
            if t >= 4:
                wait("sp", dve, ms_h1add[t - 4])
            xld[t] = op("sp", lambda E, o=x_ring[t % 4], i=x_h[t]: E.dma_start(out=o, in_=i),
                        sem=s_x[t % 4], inc=16)
            v_ln1s[t] = emit_ln(x_ring[t % 4], ln1_ring[t % 4], [(s_x[t % 4], xld[t])],
                                out_war=([(pe, v_tp1[t - 4])] if t >= 4 else ()))

        wait("pe", s_small, v_small)
        for t in (0, 1):
            p1_tile_head(t)
        for w in range(NT // 2):
            for t in (2 * w + 2, 2 * w + 3):
                if t < NT:
                    p1_tile_head(t)
            v_lnT_w = None
            for t in (2 * w, 2 * w + 1):
                v_lnT_w, v_tp1[t] = emit_transpose(
                    ln1_ring[t % 4],
                    lnT_ring[w % 2][:, :, 128 * (t % 2):128 * (t % 2 + 1)], v_ln1s[t], act)
            if w == 0:
                warmup()
            t0 = 2 * w

            def sink(tt, lo, hi, v_pe, t0=t0):
                tl = t0 + tt
                wait("dve", pe, v_pe)
                wait("dve", s_x[tl % 4], xld[tl])
                if tl >= 2:
                    wait("dve", act, ms_ln2[tl - 2])
                    if (tl - 2) % 2 == 1:
                        wait("dve", s_h1w[((tl - 2) // 2) % 2], ms_h1w[tl - 2])
                op("dve", lambda E, o=h1_ring[tl % 2][:, 0:512], a=lo[:, 0:512],
                   b=x_ring[tl % 4][:, 0:512]: E.tensor_tensor(o, a, b, ALU.add))
                v_h1 = op("dve", lambda E, o=h1_ring[tl % 2][:, 512:1024], a=hi[:, 0:512],
                          b=x_ring[tl % 4][:, 512:1024]: E.tensor_tensor(o, a, b, ALU.add),
                          sem=dve)
                ms_h1add[tl] = v_h1
                if tl % 2 == 1:
                    si = tl // 2
                    wait("sp", dve, v_h1)
                    if si >= 2:
                        wait("sp", s_h1w[si % 2], 16 * (si // 2))
                    ms_h1w[tl] = op("sp", lambda E, o=h1_sp[si], i=h1_ring[tl % 2]:
                                    E.dma_start(out=o, in_=i), sem=s_h1w[si % 2], inc=16)
                v_ln2 = emit_ln(h1_ring[tl % 2], ln2_ring[tl % 2], [(dve, v_h1)],
                                out_war=([(s_ln2w[tl % 2], ms_ln2w[tl - 2])]
                                         if tl >= 2 else ()))
                ms_ln2[tl] = v_ln2
                wait("sp", act, v_ln2)
                ms_ln2w[tl] = op("sp", lambda E, o=ln2_sp[tl], i=ln2_ring[tl % 2]:
                                 E.dma_start(out=o, in_=i), sem=s_ln2w[tl % 2], inc=16)
                return v_h1

            war_free = emit_ffn(lnT_ring[w % 2], v_lnT_w, 170.0, 168.0, sink, war_free)

        barrier()
        ln2w_tot = (s_ln2w[0].n, s_ln2w[1].n)
        h1w_tot = (s_h1w[0].n, s_h1w[1].n)

        K_sb = carve(0, [128, 8, 4096], BF16)
        V_sb = carve(64, [128, NT, NH, 65], BF16)
        kvw = carve(130.56, [128, 8, 2048], BF16)
        ln2w_ring = [carve(162.56 + 8 * i, [128, 4, D], BF16) for i in range(2)]
        ln2T_ring = [carve(178.56 + 8 * i, [128, 8, 512], BF16) for i in range(2)]

        wait("po", s_wsp, 48)
        op("po", lambda E: E.dma_start(
            out=kvw, in_=wqkvT_h[:, 1024:3072].rearrange("(c p) h -> p c h", p=128)),
            sem=sg_kvw, inc=16)
        v_vones = op("dve", lambda E: E.memset(V_sb[:, :, :, 64:65], 1.0), sem=dve)

        wait("sp", s_ln2w[0], ln2w_tot[0])
        wait("sp", s_ln2w[1], ln2w_tot[1])
        wait("act", s_wsp, 48)
        warmup()
        ms_kevac = {}
        ms_vevac = {}
        v_tr_last = {}
        for w in range(8):
            if w >= 2:
                wait("sp", pe, v_tr_last[w - 2])
            ld2 = op("sp", lambda E, o=ln2w_ring[w % 2],
                     i=ln2_sp[4 * w:4 * w + 4].rearrange("t p f -> p t f"):
                     E.dma_start(out=o, in_=i), sem=s_ln2r[w % 2], inc=16)
            v_lnT_w = None
            for tl in range(4):
                _, v_tp = emit_transpose(
                    ln2w_ring[w % 2][:, tl, :],
                    ln2T_ring[w % 2][:, :, 128 * tl:128 * (tl + 1)], ld2, s_ln2r[w % 2])
                v_tr_last[w] = v_tp
                v_lnT_w = tp_free[0]
            wait("pe", dve, v_lnT_w)
            wait("pe", sg_kvw, 16)
            wait("act", s_small, v_small)
            for j in range(8):
                bk = pb[j % 3]
                if j >= 3:
                    wait("pe", act, ms_kevac[(w, j - 3)])
                elif w > 0:
                    wait("pe", act, ms_kevac[(w - 1, j + 6 if j <= 1 else 5)])
                v_k = None
                for c in range(8):
                    v_k = op("pe", lambda E, o=bk[:, 0:512], l=kvw[:, c, 128 * j:128 * (j + 1)],
                             r=ln2T_ring[w % 2][:, c, :], st=(c == 0), sp=(c == 7):
                             E.matmul(o, l, r, start=st, stop=sp), sem=pe if c == 7 else None)
                wait("act", pe, v_k)
                ms_kevac[(w, j)] = op("act", lambda E, o=K_sb[:, j, 512 * w:512 * (w + 1)],
                                      i=bk[:, 0:512], b=bqkvc[:, 8 + j:9 + j]:
                                      E.activation(o, i, AF.Identity, bias=b[:, 0:1]),
                                      sem=act)
            for tl in range(4):
                t = 4 * w + tl
                if t >= 1:
                    wait("pe", act, ms_vevac[t - 1])
                wait("pe", s_small, v_small)
                v_v = None
                for c in range(8):
                    for hh in range(2):
                        op("pe", lambda E, o=pb[4 + hh][:, 0:512],
                           l=ln2T_ring[w % 2][:, c, 128 * tl:128 * (tl + 1)],
                           r=kvw[:, c, 1024 + 512 * hh:1024 + 512 * (hh + 1)], st=(c == 0):
                           E.matmul(o, l, r, start=st, stop=False))
                for hh in range(2):
                    v_v = op("pe", lambda E, o=pb[4 + hh][:, 0:512], l=ones_bf[0:1, 0:128],
                             r=bvr[0:1, 512 * hh:512 * (hh + 1)]:
                             E.matmul(o, l, r, start=False, stop=True),
                             sem=pe if hh == 1 else None)
                wait("act", pe, v_v)
                wait("act", dve, v_vones)
                op("act", lambda E, o=V_sb[:, t, 0:8, 0:64],
                   i=pb[4][:, 0:512].rearrange("p (a b) -> p a b", a=8):
                   E.activation(o, i, AF.Identity))
                ms_vevac[t] = op("act", lambda E, o=V_sb[:, t, 8:16, 0:64],
                                 i=pb[5][:, 0:512].rearrange("p (a b) -> p a b", a=8):
                                 E.activation(o, i, AF.Identity), sem=act)

        barrier()

        QA = carve(162.56, [128, NQ, 8, 128], BF16)
        qw = carve(130.56, [128, 8, 1024], BF16)
        lnq_ring = carve(146.56, [128, 4, D], BF16)
        lnqT = carve(154.56, [128, 8, 512], BF16)
        op("po", lambda E: E.dma_start(
            out=qw, in_=wqkvT_h[:, 0:1024].rearrange("(c p) h -> p c h", p=128)),
            sem=sg_qw, inc=16)
        ln2_odd = ln2_sp.rearrange("(a two) p f -> a two p f", two=2)
        warmup()
        ms_qevac = {}
        v_trq_last = 0
        for w in range(4):
            if w >= 1:
                wait("sp", pe, v_trq_last)
            ldq = op("sp", lambda E, o=lnq_ring,
                     i=ln2_odd[4 * w:4 * w + 4, 1:2].rearrange("t one p f -> p (t one) f"):
                     E.dma_start(out=o, in_=i), sem=s_lnqr, inc=16)
            v_lnT_w = None
            for tl in range(4):
                _, v_tp = emit_transpose(lnq_ring[:, tl, :],
                                         lnqT[:, :, 128 * tl:128 * (tl + 1)], ldq, s_lnqr)
                v_trq_last = v_tp
                v_lnT_w = tp_free[0]
            wait("pe", dve, v_lnT_w)
            wait("pe", sg_qw, 16)
            wait("act", dve, v_bq8)
            for j in range(8):
                bk = pb[j % 3]
                if j >= 3:
                    wait("pe", act, ms_qevac[(w, j - 3)])
                elif w > 0:
                    wait("pe", act, ms_qevac[(w - 1, j + 6 if j <= 1 else 5)])
                v_q = None
                for c in range(8):
                    v_q = op("pe", lambda E, o=bk[:, 0:512], l=qw[:, c, 128 * j:128 * (j + 1)],
                             r=lnqT[:, c, :], st=(c == 0), sp=(c == 7):
                             E.matmul(o, l, r, start=st, stop=sp), sem=pe if c == 7 else None)
                wait("act", pe, v_q)
                ms_qevac[(w, j)] = op("act", lambda E, o=QA[:, 4 * w:4 * w + 4, j, :],
                                      i=bk[:, 0:512].rearrange("p (a b) -> p a b", a=4),
                                      b=bq8c[:, j:j + 1]:
                                      E.activation(o, i, AF.Identity, bias=b[:, 0:1],
                                                   scale=0.125), sem=act)

        barrier()

        wo = carve(130.56, [128, 8, D], BF16)
        zbuf = [carve(146.56 + 1.0 * i, [128, 512], BF16) for i in range(4)]
        rbs_ring = [carve(150.56 + 2.0 * i, [128, 512], F32, partitions=64) for i in range(2)]
        op("po", lambda E: E.dma_start(
            out=wo, in_=woT_h.rearrange("(c p) d -> p c d", p=128)), sem=sg_wo, inc=16)

        st_lo = [pb[0], pb[1]]
        st_hi = [pb[6], pb[2]]
        a_banks = [pb[3], pb[4]]
        rbb = pb[5]
        st_rot = [0]
        z_rot = [0]
        ms_exp = {}
        ms_zcons = {}
        ms_afree = [0, 0]
        ms_rbfree = [0]

        wait("pe", s_small, v_small)
        wait("dve", s_small, v_small)
        warmup()

        def emit_qk(p, g, t, n256):
            j = g
            s0 = 2 * p
            si = st_rot[0]
            st_rot[0] += 1
            stl = st_lo[si % 2]
            sth = st_hi[si % 2]
            zi = z_rot[0]
            z_rot[0] += 1
            zb = zbuf[zi % 4]
            if si >= 2:
                wait("pe", act, ms_exp[si - 2])
            v_st = None
            for hh in range(2):
                base = 64 * hh
                if n256:
                    qs0, qs1 = s0, s0 + 2
                    ncol = 256
                else:
                    qs0, qs1 = s0 + 1, s0 + 2
                    ncol = 128
                v_st = op("pe", lambda E, o=(stl if hh == 0 else sth)[:, 0:ncol],
                          l=K_sb[base:base + 64, j, 128 * t:128 * (t + 1)],
                          r=QA[base:base + 64, qs0:qs1, j, :]:
                          E.matmul(o, l, r, start=True, stop=True),
                          sem=pe if hh == 1 else None)
            wait("act", pe, v_st)
            if zi >= 4:
                wait("act", pe, ms_zcons[zi - 4])
            half = 256 if n256 else 128
            op("act", lambda E, o=zb[:, 0:half], i=stl[:, 0:half]:
               E.activation(o, i, AF.Exp))
            v_z = op("act", lambda E, o=zb[:, half:2 * half], i=sth[:, 0:half]:
                     E.activation(o, i, AF.Exp), sem=act)
            ms_exp[si] = v_z
            gate_sem, gate_v = act, v_z
            masked = (t >= 4 * p)
            if masked:
                if n256:
                    mc = mask[:, 0:128] if t == 4 * p else mask[:, 128:256]
                    cols = [(0, 128), (256, 384)]
                else:
                    mc = mask[:, 0:128] if t == 4 * p + 2 else mask[:, 128:256]
                    cols = [(0, 128), (128, 256)]
                wait("dve", act, v_z)
                v_m = None
                for (c0, c1) in cols:
                    v_m = op("dve", lambda E, o=zb[:, c0:c1], m=mc:
                             E.tensor_tensor(o, o, m, ALU.mult),
                             sem=dve if (c0, c1) == cols[-1] else None)
                gate_sem, gate_v = dve, v_m
            return (zi, zb, gate_sem, gate_v, n256)

        def emit_sv(p, g, t, gate, first, last):
            zi, zb, gate_sem, gate_v, n256 = gate
            abank = a_banks[g % 2]
            wait("pe", gate_sem, gate_v)
            if first:
                wait("pe", dve, ms_afree[g % 2])
            v_sv = None
            for hh in range(2):
                h = 2 * g + hh
                if n256:
                    ac, zc, ncol = 256 * hh, 256 * hh, 256
                else:
                    ac, zc, ncol = 256 * hh + 128, 128 * hh, 128
                v_sv = op("pe", lambda E, o=abank[0:65, ac:ac + ncol],
                          l=V_sb[:, t, h, 0:65], r=zb[:, zc:zc + ncol],
                          st=(first and hh == 0), sp=last:
                          E.matmul(o, l, r, start=st, stop=sp),
                          sem=pe if hh == 1 else None)
            ms_zcons[zi] = v_sv
            return v_sv

        tmpb = carve(156.56, [1, 512], BF16, partitions=1)

        def emit_evac(p, g, v_sv):
            abank = a_banks[g % 2]
            ri = g % 2
            wait("dve", pe, v_sv)
            v_gh = None
            for hh in range(2):
                h = 2 * g + hh
                v_gh = op("dve", lambda E, o=tmpr[0:1, 256 * hh:256 * hh + 256],
                          i=abank[64:65, 256 * hh:256 * hh + 256], gh=ghost[0:1, h:h + 1]:
                          E.tensor_scalar(o, i, gh, None, ALU.add),
                          sem=dve if hh == 1 else None)
            wait("dve", dve, v_gh)

            def _recip_bf16(E):
                with nc.allow_low_precision(reason="normalizer feeds bf16 operand"):
                    return E.reciprocal(tmpb[0:1, 0:512], tmpr[0:1, 0:512])

            v_rr = op("dve", _recip_bf16, sem=dve)
            wait("pe", dve, v_rr)
            wait("pe", dve, ms_rbfree[0])
            v_rb = op("pe", lambda E, o=rbb[0:64, 0:512]: E.matmul(
                o, ones_bf[0:1, 0:64], tmpb[0:1, 0:512], start=True, stop=True), sem=pe)
            wait("dve", pe, v_rb)
            v_c = op("dve", lambda E, o=rbs_ring[ri], i=rbb[0:64, 0:512]:
                     E.tensor_copy(o, i), sem=dve)
            s0 = 2 * p
            v_a = None
            for hh in range(2):
                dst = QA[64 * hh:64 * hh + 64, s0:s0 + 2, g, :]
                v_a = op("dve", lambda E, o=dst,
                         a=abank[0:64, 256 * hh:256 * hh + 256].rearrange(
                             "p (a b) -> p a b", a=2),
                         b=rbs_ring[ri][0:64, 256 * hh:256 * hh + 256].rearrange(
                             "p (a b) -> p a b", a=2):
                         E.tensor_tensor(o, a, b, ALU.mult), sem=dve)
            ms_afree[g % 2] = v_a
            ms_rbfree[0] = v_c

        pairs = [(p, a) for p in range(8) for a in range(4)]
        pending = None
        for p, a in pairs:
            T = 4 * p + 4
            visits = [(t, t < 4 * p + 2) for t in range(T)]
            g0, g1 = 2 * a, 2 * a + 1
            gates0 = {0: emit_qk(p, g0, *visits[0])}
            gates1 = {0: emit_qk(p, g1, *visits[0])}
            if pending is not None:
                emit_evac(*pending[0])
                emit_evac(*pending[1])
                pending = None
            v_sv0 = v_sv1 = None
            for i in range(T):
                if i + 1 < T:
                    gates0[i + 1] = emit_qk(p, g0, *visits[i + 1])
                    gates1[i + 1] = emit_qk(p, g1, *visits[i + 1])
                t, _ = visits[i]
                v_sv0 = emit_sv(p, g0, t, gates0.pop(i), i == 0, i == T - 1)
                v_sv1 = emit_sv(p, g1, t, gates1.pop(i), i == 0, i == T - 1)
            pending = ((p, g0, v_sv0), (p, g1, v_sv1))
        emit_evac(*pending[0])
        emit_evac(*pending[1])

        barrier()

        h1in_ring = [carve(0 + 4 * i, [128, D], F32) for i in range(2)]
        x2_ring = [carve(8 + 4 * i, [128, D], F32) for i in range(2)]
        ln3_ring = [carve(146.56 + 2 * i, [128, D], BF16) for i in range(2)]
        wgt2 = {"wg": carve(16, [128, 8, HID], BF16),
                "wu": carve(64, [128, 8, HID], BF16)}
        op("sp", lambda E, o=wgt2["wg"]: E.dma_start(out=o, in_=wspg_h), sem=sg_wg, inc=16)
        op("po", lambda E, o=wgt2["wu"]: E.dma_start(out=o, in_=wspu_h), sem=sg_wu, inc=16)

        ms_x2 = {}
        ms_ln3 = {}
        ms_ln3w = {}
        wait("pe", sg_wo, 16)
        warmup()
        for s in range(NQ):
            if s >= 2:
                wait("sp", dve, ms_x2[s - 2])
                wait("sp", s_h1r[s % 2], 16 * (s // 2))
            if s == 0:
                wait("sp", s_h1w[0], h1w_tot[0])
                wait("sp", s_h1w[1], h1w_tot[1])
            ldh = op("sp", lambda E, o=h1in_ring[s % 2], i=h1_sp[s]: E.dma_start(out=o, in_=i),
                     sem=s_h1r[s % 2], inc=16)
            if s >= 1:
                wait("pe", dve, ms_x2[s - 1])
            v_wo = None
            for half in range(2):
                bank = pb[4 + half]
                for c in range(8):
                    op("pe", lambda E, o=bank[:, 0:512], l=QA[:, s, c, :],
                       r=wo[:, c, 512 * half:512 * (half + 1)], st=(c == 0):
                       E.matmul(o, l, r, start=st, stop=False))
                v_wo = op("pe", lambda E, o=bank[:, 0:512], l=ones_bf[0:1, 0:128],
                          r=bor[0:1, 512 * half:512 * (half + 1)]:
                          E.matmul(o, l, r, start=False, stop=True), sem=pe)
            wait("dve", pe, v_wo)
            wait("dve", s_h1r[s % 2], ldh)
            if s >= 2:
                wait("dve", act, ms_ln3[s - 2])
                wait("dve", s_x2w[s % 2], 16 * (s // 2))
            op("dve", lambda E, o=x2_ring[s % 2][:, 0:512], a=pb[4][:, 0:512],
               b=h1in_ring[s % 2][:, 0:512]: E.tensor_tensor(o, a, b, ALU.add))
            v_x2 = op("dve", lambda E, o=x2_ring[s % 2][:, 512:1024], a=pb[5][:, 0:512],
                      b=h1in_ring[s % 2][:, 512:1024]: E.tensor_tensor(o, a, b, ALU.add),
                      sem=dve)
            ms_x2[s] = v_x2
            wait("sp", dve, v_x2)
            op("sp", lambda E, o=x2_sp[s], i=x2_ring[s % 2]: E.dma_start(out=o, in_=i),
               sem=s_x2w[s % 2], inc=16)
            v_ln3 = emit_ln(x2_ring[s % 2], ln3_ring[s % 2], [(dve, v_x2)],
                            out_war=([(s_ln3w[s % 2], ms_ln3w[s - 2])] if s >= 2 else ()))
            ms_ln3[s] = v_ln3
            wait("sp", act, v_ln3)
            ms_ln3w[s] = op("sp", lambda E, o=ln3_sp[s], i=ln3_ring[s % 2]:
                            E.dma_start(out=o, in_=i), sem=s_ln3w[s % 2], inc=16)

        barrier()
        x2w_tot = (s_x2w[0].n, s_x2w[1].n)
        ln3w_tot = (s_ln3w[0].n, s_ln3w[1].n)

        wgt2["wd"] = carve(112, [128, 24, D], BF16)
        op("po", lambda E, o=wgt2["wd"]: E.dma_start(out=o, in_=wspd_h), sem=sg_wd, inc=16)
        wgt = wgt2
        wg_load[0] = 2
        x2in_ring = [carve(0 + 4 * i, [128, D], F32) for i in range(4)]
        out_ring = [carve(160 + 4 * i, [128, D], F32) for i in range(2)]
        ln3in_ring = [carve(168 + 4 * i, [128, 2, D], BF16) for i in range(2)]
        lnT5_ring = [carve(176 + 4 * i, [128, 8, 256], BF16) for i in range(2)]

        war_free = []
        ms_out = {}
        v_tr5 = {}
        xld5 = {}
        warmup()
        for w in range(8):
            t0 = 2 * w
            if w == 0:
                wait("sp", s_ln3w[0], ln3w_tot[0])
                wait("sp", s_ln3w[1], ln3w_tot[1])
                wait("sp", s_x2w[0], x2w_tot[0])
                wait("sp", s_x2w[1], x2w_tot[1])
            if w >= 2:
                wait("sp", pe, v_tr5[w - 2])
                wait("sp", s_ln3r[w % 2], 16 * (w // 2))
            ldl = op("sp", lambda E, o=ln3in_ring[w % 2],
                     i=ln3_sp[t0:t0 + 2].rearrange("t p f -> p t f"):
                     E.dma_start(out=o, in_=i), sem=s_ln3r[w % 2], inc=16)
            for tt in range(2):
                s = t0 + tt
                if s >= 4:
                    wait("sp", dve, ms_out[s - 4])
                    wait("sp", s_x2r[s % 4], 16 * (s // 4))
                xld5[s] = op("sp", lambda E, o=x2in_ring[s % 4], i=x2_sp[s]:
                             E.dma_start(out=o, in_=i), sem=s_x2r[s % 4], inc=16)
            v_lnT_w = None
            for tt in range(2):
                _, v_tp = emit_transpose(ln3in_ring[w % 2][:, tt, :],
                                         lnT5_ring[w % 2][:, :, 128 * tt:128 * (tt + 1)],
                                         ldl, s_ln3r[w % 2])
                v_tr5[w] = v_tp
                v_lnT_w = tp_free[0]

            def sink5(tt, lo, hi, v_pe, t0=t0):
                s = t0 + tt
                wait("dve", pe, v_pe)
                wait("dve", s_x2r[s % 4], xld5[s])
                if s >= 2:
                    wait("dve", s_out[s % 2], 16 * (s // 2))
                op("dve", lambda E, o=out_ring[s % 2][:, 0:512], a=lo[:, 0:512],
                   b=x2in_ring[s % 4][:, 0:512]: E.tensor_tensor(o, a, b, ALU.add))
                v_o = op("dve", lambda E, o=out_ring[s % 2][:, 512:1024], a=hi[:, 0:512],
                         b=x2in_ring[s % 4][:, 512:1024]: E.tensor_tensor(o, a, b, ALU.add),
                         sem=dve)
                ms_out[s] = v_o
                wait("sp", dve, v_o)
                op("sp", lambda E, o=out_h[s], i=out_ring[s % 2]:
                   E.dma_start(out=o, in_=i), sem=s_out[s % 2], inc=16)
                return v_o

            war_free = emit_ffn(lnT5_ring[w % 2], v_lnT_w, 186.0, 184.0, sink5, war_free)

        wait("sp", s_out[0], s_out[0].n)
        wait("sp", s_out[1], s_out[1].n)

        with nc.Block() as block:
            block.sync(lambda e, ops=Q["sp"]: [f(e) for f in ops])
            block.tensor(lambda e, ops=Q["pe"]: [f(e) for f in ops])
            block.scalar(lambda e, ops=Q["act"]: [f(e) for f in ops])
            block.vector(lambda e, ops=Q["dve"]: [f(e) for f in ops])
            block.gpsimd(lambda e, ops=Q["po"]: [f(e) for f in ops])

    return nc


def _stage_inputs(inputs):
    x = np.asarray(inputs["L0_attn_inputs"], dtype=np.float32)
    wqkv = np.asarray(inputs["L0_attn_Wqkv_w"], dtype=np.float32)
    bqkv = np.asarray(inputs["L0_attn_Wqkv_b"], dtype=np.float32)
    wo_w = np.asarray(inputs["L0_attn_Wo_w"], dtype=np.float32)
    bo = np.asarray(inputs["L0_attn_Wo_b"], dtype=np.float32)
    ghost = np.asarray(inputs["L0_attn_ghost"], dtype=np.float32)
    wg = np.asarray(inputs["L0_ffn_w_gate"], dtype=np.int32)
    wu = np.asarray(inputs["L0_ffn_w_up"], dtype=np.int32)
    wdn = np.asarray(inputs["L0_ffn_w_down"], dtype=np.int32)

    shared = {
        "wgT": np.ascontiguousarray(wg.T),
        "wuT": np.ascontiguousarray(wu.T),
        "wdT": np.ascontiguousarray(wdn.T),
        "wqkvT": np.ascontiguousarray(wqkv.T),
        "woT": np.ascontiguousarray(wo_w.T),
        "bqkvc": np.ascontiguousarray(bqkv.reshape(24, 128).T),
        "bvr": np.ascontiguousarray(bqkv[2048:].reshape(1, D)),
        "bor": np.ascontiguousarray(bo.reshape(1, D)),
        "ghost": np.ascontiguousarray(ghost.reshape(1, NH)),
        "eye": np.eye(128, dtype=ml_dtypes.bfloat16),
    }
    tri = np.triu(np.ones((128, 128), dtype=np.float32)).astype(ml_dtypes.bfloat16)
    ones = np.ones((128, 128), dtype=ml_dtypes.bfloat16)
    zeros = np.zeros((128, 128), dtype=ml_dtypes.bfloat16)
    mask_u = np.ascontiguousarray(np.concatenate([ones, tri], axis=1))
    mask_v = np.ascontiguousarray(np.concatenate([zeros, tri], axis=1))

    perm_v = np.arange(NT)
    perm_v[0::2] += 1
    perm_v[1::2] -= 1

    in_maps = []
    for c in range(8):
        b = c // 2
        xt = x[b].reshape(NT, 128, D)
        if c % 2 == 1:
            xt = xt[perm_v]
        m = dict(shared)
        m["x"] = np.ascontiguousarray(xt)
        m["mask"] = mask_u if c % 2 == 0 else mask_v
        in_maps.append(m)
    return in_maps


def _assemble(results):
    full = np.empty((4, NT * 128, D), dtype=np.float32)
    for c in range(8):
        b = c // 2
        o = np.asarray(results[c]["out"], dtype=np.float32)
        for s in range(NQ):
            j = 2 * s + 1 if c % 2 == 0 else 2 * s
            full[b, 128 * j:128 * (j + 1), :] = o[s]
    return full


def kernel(**inputs):
    nc = build_program()
    in_maps = _stage_inputs(inputs)
    res = run_bass_kernel_spmd(nc, in_maps, core_ids=list(range(8)))
    return _assemble(res.results)
